# version 04
# baseline (speedup 1.0000x reference)
import numpy as np
from contextlib import ExitStack

import concourse.bass as bass
import concourse.mybir as mybir
from concourse.bass_utils import run_bass_kernel_spmd

F32 = mybir.dt.float32
F16 = mybir.dt.float16
AX = mybir.AxisListType.X
OP = mybir.AluOpType
ACT = mybir.ActivationFunctionType

B, S, D = 2, 4096, 2048
H, KVH, DH = 16, 4, 128
HID = 8192
NT = 8
NKT = S // 128
EPS_RMS = 1e-8
EPS_LN = 1e-5


class Sem:
    def __init__(self, nc, ctx, name, step=1):
        self.h = ctx.enter_context(nc.semaphore(name))
        self.n = 0
        self.step = step

    def tick(self, instr, k=1):
        self.n += k * self.step
        instr.then_inc(self.h, k * self.step)
        return self.n


def build_program():
    nc = bass.Bass("TRN2", target_bir_lowering=False,
                   dynamic_dma_scratch_size=2048)

    x_d = nc.dram_tensor("x", [S, D], F32, kind="ExternalInput").ap()
    xq_d = nc.dram_tensor("xq", [NT * 128, D], F32, kind="ExternalInput").ap()
    xqT_d = nc.dram_tensor("xqT", [D, NT * 128], F32, kind="ExternalInput").ap()
    wq_d = nc.dram_tensor("wq", [D, D], F32, kind="ExternalInput").ap()
    wk_d = nc.dram_tensor("wk", [D, KVH * DH], F32, kind="ExternalInput").ap()
    wv_d = nc.dram_tensor("wv", [D, KVH * DH], F32, kind="ExternalInput").ap()
    wo_d = nc.dram_tensor("wo", [D, D], F32, kind="ExternalInput").ap()
    wgT_d = nc.dram_tensor("wgT", [D, HID], F32, kind="ExternalInput").ap()
    wuT_d = nc.dram_tensor("wuT", [D, HID], F32, kind="ExternalInput").ap()
    wdT_d = nc.dram_tensor("wdT", [HID, D], F32, kind="ExternalInput").ap()
    gqb_d = nc.dram_tensor("gqb", [128, D], F32, kind="ExternalInput").ap()
    gkb_d = nc.dram_tensor("gkb", [128, KVH * DH], F32, kind="ExternalInput").ap()
    dmask_d = nc.dram_tensor("dmask", [NT, 128, 512], F32, kind="ExternalInput").ap()
    out_d = nc.dram_tensor("out", [NT * 128, D], F32, kind="ExternalOutput").ap()

    ident16 = nc.inline_tensor(np.eye(128, dtype=np.float16), "ident16").ap()
    ident32 = nc.inline_tensor(np.eye(128, dtype=np.float32), "ident32").ap()

    TE, VE, SE, SY, GP = nc.tensor, nc.vector, nc.scalar, nc.sync, nc.gpsimd

    wait_memo = {}

    def wt(eng, sem, val):
        if val <= 0:
            return
        key = (id(eng), id(sem))
        if wait_memo.get(key, 0) < val:
            eng.wait_ge(sem.h, val)
            wait_memo[key] = val

    def mk_chains(te, ve, se, TE, VE, SE, wt):
        def vch(instr):
            v = ve.tick(instr)
            wt(VE, ve, v)
            return v

        def sch(instr):
            v = se.tick(instr)
            wt(SE, se, v)
            return v
        return vch, sch

    with ExitStack() as ctx0:
        e0 = ctx0.enter_context
        te = Sem(nc, ctx0, "te")
        ve = Sem(nc, ctx0, "ve")
        se = Sem(nc, ctx0, "se")
        dg = Sem(nc, ctx0, "dg", 16)
        sst = [Sem(nc, ctx0, f"sst{i}", 16) for i in range(2)]
        sxr = [Sem(nc, ctx0, f"sxr{i}", 16) for i in range(2)]
        swkv = [Sem(nc, ctx0, f"swkv{i}", 16) for i in range(2)]
        swq = [Sem(nc, ctx0, f"swq{i}", 16) for i in range(2)]
        sxq = Sem(nc, ctx0, "sxq", 16)
        sdm = [Sem(nc, ctx0, f"sdm{i}", 16) for i in range(2)]
        swo = [Sem(nc, ctx0, f"swo{i}", 16) for i in range(2)]
        sxt = [Sem(nc, ctx0, f"sxt{i}", 16) for i in range(2)]
        swgu = [Sem(nc, ctx0, f"swgu{i}", 16) for i in range(2)]
        swd = Sem(nc, ctx0, "swd", 16)

        vch, sch = mk_chains(te, ve, se, TE, VE, SE, wt)
        id16 = e0(nc.sbuf_tensor("id16", [128, 128], F16))
        id32 = e0(nc.sbuf_tensor("id32", [128, 128], F32))
        cst = e0(nc.sbuf_tensor("cst", [128, 4], F32))
        gqb = e0(nc.sbuf_tensor("gqb_s", [128, D], F16))
        gkb = e0(nc.sbuf_tensor("gkb_s", [128, KVH * DH], F16))

        with ExitStack() as ctxY, ExitStack() as ctxZ, ExitStack() as ctxA:
            yT = ctxY.enter_context(nc.sbuf_tensor("yT", [128, 16, NT * 128], F16))
            zT = ctxZ.enter_context(nc.sbuf_tensor("zT", [128, H, NT * 128], F16))
            eA = ctxA.enter_context
            kT = eA(nc.sbuf_tensor("kT", [128, KVH, S], F16))
            vsb = eA(nc.sbuf_tensor("vsb", [128, NKT, KVH * DH], F16))

            _scid_P1_kv = nc.enter_named_scope("P1_kv", False)[0]
            with ExitStack() as c1:
                e1 = c1.enter_context
                wkv = e1(nc.sbuf_tensor("wkv", [128, 16, 1024], F16))
                wstg = [e1(nc.sbuf_tensor(f"wstg{i}", [128, 1024], F32)) for i in range(2)]
                xr32 = [e1(nc.sbuf_tensor(f"xr32{i}", [128, D], F32)) for i in range(2)]
                xr16 = e1(nc.sbuf_tensor("xr16", [128, D], F16))
                xTt = [e1(nc.sbuf_tensor(f"xTt{i}", [128, 16, 128], F16)) for i in range(2)]
                khat = e1(nc.sbuf_tensor("khat", [128, 512], F16))
                sqk = e1(nc.sbuf_tensor("sqk", [128, 512], F32))
                msk = e1(nc.sbuf_tensor("msk", [128, 4], F32))
                sdk = e1(nc.sbuf_tensor("sdk", [128, 4], F32))
                rk = e1(nc.sbuf_tensor("rk", [128, 4], F32))
                psk = [e1(nc.psum_tensor(f"psk{i}", [128, 512], F32)) for i in range(2)]
                psv = [e1(nc.psum_tensor(f"psv{i}", [128, 512], F32)) for i in range(2)]
                ptk = e1(nc.psum_tensor("ptk", [128, 512], F16))
                ptx = [e1(nc.psum_tensor(f"ptx{i}", [128, 512], F16)) for i in range(2)]
                pjk1 = e1(nc.psum_tensor("pjk1", [128, 128], F16))

                VE.memset(cst[:, 0:1], EPS_RMS)
                VE.memset(cst[:, 1:2], 1.0 / DH)
                VE.memset(cst[:, 2:3], DH * EPS_RMS)
                cs_done = ve.tick(VE.memset(cst[:, 3:4], EPS_LN))

                dg.tick(SY.dma_start(id16[:], ident16))
                dg.tick(SY.dma_start(id32[:], ident32))
                dg.tick(SY.dma_start(wstg[0][:], gqb_d[:, 0:1024]))
                dg.tick(SY.dma_start(wstg[1][:], gqb_d[:, 1024:2048]))
                idr = dg.tick(SY.dma_start(xr32[0][:, 0:512], gkb_d))
                wt(VE, dg, idr)
                VE.tensor_copy(gqb[:, 0:1024], wstg[0][:])
                VE.tensor_copy(gqb[:, 1024:2048], wstg[1][:])
                gam_done = ve.tick(VE.tensor_copy(gkb[:], xr32[0][:, 0:512]))

                wkv_cast = [0, 0]
                wr = [0, 0]

                def issue_wkv(dc):
                    p = dc % 2
                    wt(SY, ve, wkv_cast[p])
                    swkv[p].tick(SY.dma_start(wstg[p][:, 0:512],
                                              wk_d[128 * dc:128 * (dc + 1), :]))
                    wr[p] = swkv[p].tick(
                        SY.dma_start(wstg[p][:, 512:1024],
                                     wv_d[128 * dc:128 * (dc + 1), :]))

                wt(SY, ve, gam_done)
                issue_wkv(0)
                for dc in range(16):
                    p = dc % 2
                    if dc + 1 < 16:
                        issue_wkv(dc + 1)
                    wt(VE, swkv[p], wr[p])
                    wkv_cast[p] = ve.tick(VE.tensor_copy(wkv[:, dc, :], wstg[p][:]))
                wkv_ready = max(wkv_cast)

                xr_ready = [0, 0]
                cast_done = [0, 0]
                xr16_free = 0
                xt_used = [0, 0]
                ptx_ev = [0, 0]
                se_apply_done = [0, 0]
                ve_vcopy_done = [0, 0]
                kt_ev_done = 0

                wt(SE, ve, gam_done)
                xr_ready[0] = sxr[0].tick(SE.dma_start(xr32[0][:], x_d[0:128, :]))
                for kt in range(NKT):
                    p = kt % 2
                    if kt + 1 < NKT:
                        pn = (kt + 1) % 2
                        wt(SE, ve, cast_done[pn])
                        xr_ready[pn] = sxr[pn].tick(
                            SE.dma_start(xr32[pn][:], x_d[128 * (kt + 1):128 * (kt + 2), :]))
                    wt(VE, sxr[p], xr_ready[p])
                    wt(VE, te, xr16_free)
                    cast_done[p] = ve.tick(VE.tensor_copy(xr16[:], xr32[p][:]))
                    wt(TE, ve, cast_done[p])
                    wt(TE, dg, idr)
                    tks = []
                    for grp in range(4):
                        gp_ = grp % 2
                        wt(TE, se, ptx_ev[gp_])
                        tr = None
                        for u in range(4):
                            dc = 4 * grp + u
                            tr = TE.transpose(ptx[gp_][:, 128 * u:128 * (u + 1)],
                                              xr16[:, 128 * dc:128 * (dc + 1)], id16[:])
                        tks.append(te.tick(tr))
                        if grp >= 1:
                            wt(SE, te, tks[grp])
                            wt(SE, te, xt_used[p])
                            gq_ = (grp - 1) % 2
                            ptx_ev[gq_] = se.tick(
                                SE.copy(xTt[p][:, 4 * (grp - 1):4 * grp, :], ptx[gq_][:]))
                    tks.append(te.tick(TE.transpose(pjk1[:], id16[:], id16[:])))
                    wt(SE, te, tks[4])
                    wt(SE, te, xt_used[p])
                    ptx_ev[1] = se.tick(
                        SE.copy(xTt[p][:, 12:16, :], ptx[1][:]))
                    xr16_free = te.n
                    xt_ev = se.n
                    wt(TE, se, se_apply_done[p])
                    wt(TE, ve, ve_vcopy_done[p])
                    wt(TE, ve, wkv_ready)
                    wt(TE, se, xt_ev)
                    mmv = None
                    for dc in range(16):
                        lhsT = xTt[p][:, dc, :]
                        TE.matmul(psk[p][:], lhsT, wkv[:, dc, 0:512],
                                  start=(dc == 0), stop=(dc == 15))
                        mmv = TE.matmul(psv[p][:], lhsT, wkv[:, dc, 512:1024],
                                        start=(dc == 0), stop=(dc == 15))
                    mm_done = te.tick(mmv)
                    xt_used[p] = mm_done
                    wt(SE, te, mm_done)
                    sq_done = se.tick(SE.activation(sqk[:], psk[p][:], ACT.Square))
                    wt(VE, se, sq_done)
                    rs = None
                    for s4 in range(4):
                        rs = VE.reduce_sum(msk[:, s4:s4 + 1],
                                           sqk[:, 128 * s4:128 * (s4 + 1)], axis=AX)
                    red_done = ve.tick(rs)
                    wt(SE, ve, red_done)
                    wt(SE, ve, cs_done)
                    sd_done = se.tick(SE.activation(sdk[:], msk[:], ACT.Sqrt,
                                                    bias=cst[:, 0:1], scale=cst[:, 1:2]))
                    wt(VE, se, sd_done)
                    rk_done = ve.tick(VE.reciprocal(rk[:], sdk[:]))
                    wt(SE, ve, rk_done)
                    ap = None
                    for s4 in range(4):
                        ap = SE.mul(khat[:, 128 * s4:128 * (s4 + 1)],
                                    psk[p][:, 128 * s4:128 * (s4 + 1)], rk[:, s4:s4 + 1])
                    se_apply_done[p] = se.tick(ap)
                    wt(VE, se, se_apply_done[p])
                    kg_done = ve.tick(VE.tensor_mul(khat[:], khat[:], gkb[:]))
                    wt(TE, ve, kg_done)
                    wt(TE, se, kt_ev_done)
                    for s4 in range(4):
                        TE.transpose(ptk[:, 128 * s4:128 * (s4 + 1)],
                                     khat[:, 128 * s4:128 * (s4 + 1)], id16[:])
                    tr_done = te.tick(TE.transpose(pjk1[:], id16[:], id16[:]))
                    wt(SE, te, tr_done)
                    cp = None
                    for s4 in range(4):
                        cp = SE.copy(kT[:, s4, 128 * kt:128 * (kt + 1)],
                                     ptk[:, 128 * s4:128 * (s4 + 1)])
                    kt_ev_done = se.tick(cp)
                    wt(VE, te, mm_done)
                    ve_vcopy_done[p] = ve.tick(VE.tensor_copy(vsb[:, kt, :], psv[p][:]))

            nc.leave_named_scope("P1_kv", _scid_P1_kv, False)
            _scid_P2_q = nc.enter_named_scope("P2_q", False)[0]
            for _E in (SY, SE):
                wt(_E, te, te.n)
                wt(_E, ve, ve.n)
                wt(_E, se, se.n)
            qT = eA(nc.sbuf_tensor("qT", [128, H, NT * 128], F16))
            with ExitStack() as c2:
                e2 = c2.enter_context
                qh16 = e2(nc.sbuf_tensor("qh16", [128, 1024], F16))
                wqs = [e2(nc.sbuf_tensor(f"wqs{i}", [128, 1024], F32)) for i in range(2)]
                xqs = e2(nc.sbuf_tensor("xqs", [128, 16, 256], F32))
                sqq = e2(nc.sbuf_tensor("sqq", [128, 1024], F32))
                msq = e2(nc.sbuf_tensor("msq", [128, 8], F32))
                sdq = e2(nc.sbuf_tensor("sdq", [128, 8], F32))
                rq = e2(nc.sbuf_tensor("rq", [128, 8], F32))
                psq = [e2(nc.psum_tensor(f"psq{i}", [128, 1024], F32)) for i in range(2)]
                ptq = e2(nc.psum_tensor("ptq", [128, 512], F16))
                pjk2 = e2(nc.psum_tensor("pjk2", [128, 128], F16))

                xqs_ready = 0
                xqs_freed = 0
                wq_rd = [0, 0]
                wq_freed = [0, 0]
                se_qapply_done = [0, 0]
                ptq_ev = 0
                mmq_done = {}

                def issue_wq(fh, dc):
                    i = dc % 2
                    wt(SY, te, wq_freed[i])
                    wq_rd[i] = swq[i].tick(
                        SY.dma_start(wqs[i][:],
                                     wq_d[128 * dc:128 * (dc + 1),
                                          1024 * fh:1024 * (fh + 1)]))

                for p4 in range(4):
                    wt(SE, te, xqs_freed)
                    r = 0
                    for dc in range(16):
                        r = sxq.tick(
                            SE.dma_start(xqs[:, dc, :],
                                         xqT_d[128 * dc:128 * (dc + 1),
                                               256 * p4:256 * (p4 + 1)]))
                    xqs_ready = r
                    for fh in range(2):
                        issue_wq(fh, 0)
                        for dc in range(16):
                            if dc + 1 < 16:
                                wq_freed[(dc + 1) % 2] = mmq_done.get(dc - 1, 0)
                                issue_wq(fh, dc + 1)
                            wt(TE, swq[dc % 2], wq_rd[dc % 2])
                            wt(TE, sxq, xqs_ready)
                            mq = None
                            for j in range(2):
                                wt(TE, se, se_qapply_done[j])
                                lhsT = xqs[:, dc, 128 * j:128 * (j + 1)]
                                for nb in range(2):
                                    mq = TE.matmul(psq[j][:, 512 * nb:512 * (nb + 1)],
                                                   lhsT,
                                                   wqs[dc % 2][:, 512 * nb:512 * (nb + 1)],
                                                   start=(dc == 0), stop=(dc == 15))
                            mmq_done[dc] = te.tick(mq)
                        wq_freed = [mmq_done[15], mmq_done[15]]
                        for j in range(2):
                            sl = 2 * p4 + j
                            wt(SE, te, mmq_done[15])
                            sq_done = se.tick(SE.activation(sqq[:], psq[j][:], ACT.Square))
                            wt(VE, se, sq_done)
                            rs = None
                            for s8 in range(8):
                                rs = VE.reduce_sum(msq[:, s8:s8 + 1],
                                                   sqq[:, 128 * s8:128 * (s8 + 1)], axis=AX)
                            red_done = ve.tick(rs)
                            wt(SE, ve, red_done)
                            sd_done = se.tick(SE.activation(sdq[:], msq[:], ACT.Sqrt,
                                                            bias=cst[:, 2:3], scale=1.0))
                            wt(VE, se, sd_done)
                            rq_done = ve.tick(VE.reciprocal(rq[:], sdq[:]))
                            wt(SE, ve, rq_done)
                            ap = None
                            for s8 in range(8):
                                ap = SE.mul(qh16[:, 128 * s8:128 * (s8 + 1)],
                                            psq[j][:, 128 * s8:128 * (s8 + 1)],
                                            rq[:, s8:s8 + 1])
                            se_qapply_done[j] = se.tick(ap)
                            wt(VE, se, se_qapply_done[j])
                            qg_done = ve.tick(VE.tensor_mul(
                                qh16[:], qh16[:], gqb[:, 1024 * fh:1024 * (fh + 1)]))
                            wt(TE, ve, qg_done)
                            for grp in range(2):
                                wt(TE, se, ptq_ev)
                                tr = None
                                for u in range(4):
                                    s8 = 4 * grp + u
                                    tr = TE.transpose(ptq[:, 128 * u:128 * (u + 1)],
                                                      qh16[:, 128 * s8:128 * (s8 + 1)],
                                                      id16[:])
                                te.tick(tr)
                                trd = te.tick(TE.transpose(pjk2[:], id16[:], id16[:]))
                                wt(SE, te, trd)
                                cp = None
                                for u in range(4):
                                    hh = 8 * fh + 4 * grp + u
                                    cp = SE.copy(qT[:, hh, 128 * sl:128 * (sl + 1)],
                                                 ptq[:, 128 * u:128 * (u + 1)])
                                ptq_ev = se.tick(cp)
                    xqs_freed = te.n

            nc.leave_named_scope("P2_q", _scid_P2_q, False)
            _scid_P3_attn = nc.enter_named_scope("P3_attn", False)[0]
            for _E in (SY, SE):
                wt(_E, te, te.n)
                wt(_E, ve, ve.n)
                wt(_E, se, se.n)
            p23_se = se.n
            p23_ve = ve.n
            wt(TE, se, p23_se)
            wt(TE, ve, p23_ve)
            with ExitStack() as c3:
                e3 = c3.enter_context
                scr = [e3(nc.sbuf_tensor(f"scr{i}", [128, 4096], F32)) for i in range(2)]
                ph = [e3(nc.sbuf_tensor(f"ph{i}", [128, 4096], F16)) for i in range(2)]
                pt4 = [e3(nc.sbuf_tensor(f"pt4{i}", [128, 512], F16)) for i in range(2)]
                dmb = [e3(nc.sbuf_tensor(f"dmb{i}", [128, 512], F32)) for i in range(2)]
                nm = [e3(nc.sbuf_tensor(f"nm{i}", [128, 1], F32)) for i in range(2)]
                ell = [e3(nc.sbuf_tensor(f"ell{i}", [128, 1], F32)) for i in range(2)]
                lnl = [e3(nc.sbuf_tensor(f"lnl{i}", [128, 1], F32)) for i in range(2)]
                nm2 = [e3(nc.sbuf_tensor(f"nm2{i}", [128, 1], F32)) for i in range(2)]
                pss = [e3(nc.psum_tensor(f"pss{i}", [128, 512], F32)) for i in range(2)]
                pst = [e3(nc.psum_tensor(f"pst{i}", [128, 512], F16)) for i in range(2)]
                pso = [e3(nc.psum_tensor(f"pso{i}", [128, 128], F32)) for i in range(2)]
                pjk3 = e3(nc.psum_tensor("pjk3", [128, 128], F16))

                NIT = NT * H
                def it_gh(i):
                    return i // H, i % H

                sc = [0, 0]
                ve_scr_done = [0, 0]
                nm_done = [0, 0]
                ex_done = [0, 0]
                te_ph_done = [0, 0]
                se_pt_done = [0, 0]
                zev_done = [0, 0]
                dm_ready = [0, 0]
                dm_used = [0, 0]
                dm_ready[0] = sdm[0].tick(SE.dma_start(dmb[0][:], dmask_d[0]))
                dm_issued = 0

                def emit_stats(i):
                    g, h = it_gh(i)
                    nb = g + 1
                    rho = i % 2
                    g4 = h // 4
                    for kb in range(nb):
                        p = kb % 2
                        wt(TE, se, sc[p])
                        mm = TE.matmul(pss[p][:], qT[:, h, 128 * g:128 * (g + 1)],
                                       kT[:, g4, 512 * kb:512 * (kb + 1)],
                                       start=True, stop=True)
                        mmd = te.tick(mm)
                        wt(SE, te, mmd)
                        wt(SE, ve, ve_scr_done[rho])
                        sc[p] = se.tick(SE.copy(scr[rho][:, 512 * kb:512 * (kb + 1)],
                                                pss[p][:]))

                def emit_softmax(i):
                    g, h = it_gh(i)
                    nb = g + 1
                    rho = i % 2
                    wt(VE, se, max(sc))
                    wt(VE, sdm[g % 2], dm_ready[g % 2])
                    vch(VE.tensor_tensor(scr[rho][:, 512 * g:512 * (g + 1)],
                                         scr[rho][:, 512 * g:512 * (g + 1)],
                                         dmb[g % 2][:], op=OP.min))
                    nm_done[rho] = vch(VE.tensor_reduce(nm[rho][:],
                                                        scr[rho][:, 0:512 * nb],
                                                        axis=AX, op=OP.max, negate=True))
                    ve_scr_done[rho] = nm_done[rho]
                    dm_used[g % 2] = nm_done[rho]
                    wt(SE, ve, nm_done[rho])
                    wt(SE, te, te_ph_done[rho])
                    sch(SE.activation(ph[rho][:, 0:512 * nb], scr[rho][:, 0:512 * nb],
                                      ACT.Exp, bias=nm[rho][:], accum_out=ell[rho][:]))
                    l_done = sch(SE.activation(lnl[rho][:], ell[rho][:], ACT.Ln))
                    wt(VE, se, l_done)
                    wt(VE, ve, nm_done[rho])
                    n2 = ve.tick(VE.tensor_sub(nm2[rho][:], nm[rho][:], lnl[rho][:]))
                    wt(SE, ve, n2)
                    ex_done[rho] = se.tick(SE.activation(ph[rho][:, 0:512 * nb],
                                                         scr[rho][:, 0:512 * nb],
                                                         ACT.Exp, bias=nm2[rho][:]))

                def emit_tav(i):
                    g, h = it_gh(i)
                    nb = g + 1
                    rho = i % 2
                    g4 = h // 4
                    EV = SE if (i % 2 == 0) else VE
                    evs = se if (i % 2 == 0) else ve
                    wt(TE, se, ex_done[rho])
                    wt(TE, se if zev_eng[rho] == 0 else ve, zev_done[rho])

                    def emit_tchunk(q4):
                        pp = q4 % 2
                        wt(TE, se, se_pt_done[pp] if se_pt_eng[pp] == 0 else 0)
                        wt(TE, ve, se_pt_done[pp] if se_pt_eng[pp] == 1 else 0)
                        tr = None
                        for u in range(4):
                            tr = TE.transpose(pst[pp][:, 128 * u:128 * (u + 1)],
                                              ph[rho][:, 512 * q4 + 128 * u:
                                                       512 * q4 + 128 * (u + 1)], id16[:])
                        return te.tick(tr)

                    tch = [0] * (nb + 1)
                    tch[0] = emit_tchunk(0)
                    av_done = 0
                    for q4 in range(nb):
                        if q4 + 1 < nb:
                            tch[q4 + 1] = emit_tchunk(q4 + 1)
                        else:
                            tch[q4 + 1] = te.tick(
                                TE.transpose(pjk3[:], id16[:], id16[:]))
                        wt(EV, te, tch[q4 + 1])
                        if EV is SE:
                            se_pt_done[q4 % 2] = evs.tick(
                                SE.copy(pt4[q4 % 2][:], pst[q4 % 2][:]))
                            se_pt_eng[q4 % 2] = 0
                        else:
                            se_pt_done[q4 % 2] = evs.tick(
                                VE.tensor_copy(pt4[q4 % 2][:], pst[q4 % 2][:]))
                            se_pt_eng[q4 % 2] = 1
                        wt(TE, se if se_pt_eng[q4 % 2] == 0 else ve,
                           se_pt_done[q4 % 2])
                        av = None
                        for u in range(4):
                            ktile = 4 * q4 + u
                            av = TE.matmul(pso[rho][:],
                                           vsb[:, ktile, 128 * g4:128 * (g4 + 1)],
                                           pt4[q4 % 2][:, 128 * u:128 * (u + 1)],
                                           start=(ktile == 0),
                                           stop=(ktile == 4 * nb - 1))
                        av_done = te.tick(av)
                    te_ph_done[rho] = te.n
                    wt(EV, te, av_done)
                    if EV is SE:
                        zev_done[rho] = evs.tick(
                            SE.copy(zT[:, h, 128 * g:128 * (g + 1)], pso[rho][:]))
                        zev_eng[rho] = 0
                    else:
                        zev_done[rho] = evs.tick(
                            VE.tensor_copy(zT[:, h, 128 * g:128 * (g + 1)], pso[rho][:]))
                        zev_eng[rho] = 1

                se_pt_eng = [0, 0]
                zev_eng = [0, 0]
                emit_stats(0)
                for i in range(NIT):
                    g, h = it_gh(i)
                    if h == 0 and g + 1 < NT and dm_issued < g + 1:
                        pn = (g + 1) % 2
                        wt(SE, ve, dm_used[pn])
                        dm_ready[pn] = sdm[pn].tick(
                            SE.dma_start(dmb[pn][:], dmask_d[g + 1]))
                        dm_issued = g + 1
                    if i + 1 < NIT:
                        emit_stats(i + 1)
                    emit_softmax(i)
                    emit_tav(i)
            zev_all = se.n
            ctxA.close()

            nc.leave_named_scope("P3_attn", _scid_P3_attn, False)
            _scid_P4_wo = nc.enter_named_scope("P4_wo", False)[0]
            for _E in (SY, SE):
                wt(_E, te, te.n)
                wt(_E, ve, ve.n)
                wt(_E, se, se.n)
            with ExitStack() as c4:
                e4 = c4.enter_context
                wo16 = e4(nc.sbuf_tensor("wo16", [128, 16, D], F16))
                wos = [e4(nc.sbuf_tensor(f"wos{i}", [128, D], F32)) for i in range(2)]
                xts = [e4(nc.sbuf_tensor(f"xts{i}", [128, D], F32)) for i in range(2)]
                rbuf = e4(nc.sbuf_tensor("rbuf", [128, D], F32))
                jnk = e4(nc.sbuf_tensor("jnk", [128, D], F32))
                ytile = e4(nc.sbuf_tensor("ytile", [128, D], F16))
                s1 = e4(nc.sbuf_tensor("s1", [128, 1], F32))
                s2 = e4(nc.sbuf_tensor("s2", [128, 1], F32))
                mu = e4(nc.sbuf_tensor("mu", [128, 1], F32))
                m2m = e4(nc.sbuf_tensor("m2m", [128, 1], F32))
                msq2 = e4(nc.sbuf_tensor("msq2", [128, 1], F32))
                var = e4(nc.sbuf_tensor("var", [128, 1], F32))
                sd = e4(nc.sbuf_tensor("sd", [128, 1], F32))
                rstd = e4(nc.sbuf_tensor("rstd", [128, 1], F32))
                nmr = e4(nc.sbuf_tensor("nmr", [128, 1], F32))
                pswo = e4(nc.psum_tensor("pswo", [128, D], F32))
                psyt = [e4(nc.psum_tensor(f"psyt{i}", [128, 512], F16)) for i in range(2)]
                pjk4 = e4(nc.psum_tensor("pjk4", [128, 128], F16))

                wo_cast = [0, 0]
                wo_rd = [0, 0]

                def issue_wo(dc):
                    p = dc % 2
                    wt(SY, ve, wo_cast[p])
                    wo_rd[p] = swo[p].tick(
                        SY.dma_start(wos[p][:], wo_d[128 * dc:128 * (dc + 1), :]))

                issue_wo(0)
                for dc in range(16):
                    p = dc % 2
                    if dc + 1 < 16:
                        issue_wo(dc + 1)
                    wt(VE, swo[p], wo_rd[p])
                    wo_cast[p] = ve.tick(VE.tensor_copy(wo16[:, dc, :], wos[p][:]))
                wo_all = max(wo_cast)

                xq_ready = [0, 0]
                xq_ready[0] = sxt[0].tick(SE.dma_start(xts[0][:], xq_d[0:128, :]))

                ve_r_done = 0
                yt_te_done = 0
                yt_ev = [0, 0]
                for g in range(NT):
                    if g + 1 < NT:
                        pn = (g + 1) % 2
                        wt(SE, ve, ve_r_done)
                        xq_ready[pn] = sxt[pn].tick(
                            SE.dma_start(xts[pn][:], xq_d[128 * (g + 1):128 * (g + 2), :]))
                    wt(TE, ve, wo_all)
                    wt(TE, se, zev_all)
                    wt(TE, ve, ve_r_done)
                    mo = None
                    for hh in range(16):
                        for nbk in range(4):
                            mo = TE.matmul(pswo[:, 512 * nbk:512 * (nbk + 1)],
                                           zT[:, hh, 128 * g:128 * (g + 1)],
                                           wo16[:, hh, 512 * nbk:512 * (nbk + 1)],
                                           start=(hh == 0), stop=(hh == 15))
                    wo_mm = te.tick(mo)
                    wt(VE, te, wo_mm)
                    wt(VE, sxt[g % 2], xq_ready[g % 2])
                    vch(VE.scalar_tensor_tensor(rbuf[:], xts[g % 2][:], 2.0, pswo[:],
                                                op0=OP.mult, op1=OP.add))
                    vch(VE.tensor_scalar(jnk[:], rbuf[:], 1.0, 0.0, op0=OP.mult,
                                         op1=OP.add, accum_out=s1[:]))
                    vch(VE.scalar_tensor_tensor(jnk[:], rbuf[:], 1.0, rbuf[:],
                                                op0=OP.mult, op1=OP.mult, accum_out=s2[:]))
                    vch(VE.tensor_scalar(mu[:], s1[:], 1.0 / D, 0.0, op0=OP.mult, op1=OP.add))
                    vch(VE.tensor_scalar(m2m[:], s2[:], 1.0 / D, 0.0, op0=OP.mult, op1=OP.add))
                    vch(VE.tensor_mul(msq2[:], mu[:], mu[:]))
                    ve_r_done = ve.tick(VE.tensor_sub(var[:], m2m[:], msq2[:]))
                    wt(SE, ve, ve_r_done)
                    sd_done = se.tick(SE.activation(sd[:], var[:], ACT.Sqrt,
                                                    bias=cst[:, 3:4]))
                    wt(VE, se, sd_done)
                    vch(VE.reciprocal(rstd[:], sd[:]))
                    nmr_done = vch(VE.tensor_scalar(nmr[:], mu[:], rstd[:, 0:1], -1.0,
                                                    op0=OP.mult, op1=OP.mult))
                    wt(VE, te, yt_te_done)
                    ve_y_done = ve.tick(VE.tensor_scalar(ytile[:], rbuf[:],
                                                         rstd[:, 0:1], nmr[:, 0:1],
                                                         op0=OP.mult, op1=OP.add))
                    wt(TE, ve, ve_y_done)
                    ytk = []
                    for grp in range(4):
                        pp = grp % 2
                        wt(TE, se, yt_ev[pp])
                        tr = None
                        for u in range(4):
                            cc = 4 * grp + u
                            tr = TE.transpose(psyt[pp][:, 128 * u:128 * (u + 1)],
                                              ytile[:, 128 * cc:128 * (cc + 1)], id16[:])
                        ytk.append(te.tick(tr))
                        if grp >= 1:
                            wt(SE, te, ytk[grp])
                            cp = None
                            for u2 in range(4):
                                c0 = 4 * (grp - 1) + u2
                                cp = SE.copy(yT[:, c0, 128 * g:128 * (g + 1)],
                                             psyt[(grp - 1) % 2][:, 128 * u2:128 * (u2 + 1)])
                            yt_ev[(grp - 1) % 2] = se.tick(cp)
                    ytk.append(te.tick(TE.transpose(pjk4[:], id16[:], id16[:])))
                    wt(SE, te, ytk[4])
                    cp = None
                    for u2 in range(4):
                        c0 = 12 + u2
                        cp = SE.copy(yT[:, c0, 128 * g:128 * (g + 1)],
                                     psyt[1][:, 128 * u2:128 * (u2 + 1)])
                    yt_ev[1] = se.tick(cp)
                    yt_te_done = te.n
                yT_all = se.n
            ctxZ.close()

            nc.leave_named_scope("P4_wo", _scid_P4_wo, False)
            _scid_P5_ffn = nc.enter_named_scope("P5_ffn", False)[0]
            for _E in (SY, SE):
                wt(_E, te, te.n)
                wt(_E, ve, ve.n)
                wt(_E, se, se.n)
            f = ctxY.enter_context(nc.sbuf_tensor("f", [128, NT, D], F32))
            with ExitStack() as c5m:
                e5 = c5m.enter_context
                gatedT = e5(nc.sbuf_tensor("gatedT", [128, 8, 1024], F16))
                wd16 = e5(nc.sbuf_tensor("wd16", [128, 8, D], F16))
                gsil = e5(nc.sbuf_tensor("gsil", [128, 1024], F16))
                wgs32 = [e5(nc.sbuf_tensor(f"wgs32{i}", [128, 16, 128], F32)) for i in range(2)]
                wus32 = [e5(nc.sbuf_tensor(f"wus32{i}", [128, 16, 128], F32)) for i in range(2)]
                wgs16 = [e5(nc.sbuf_tensor(f"wgs16{i}", [128, 16, 128], F16)) for i in range(2)]
                wus16 = [e5(nc.sbuf_tensor(f"wus16{i}", [128, 16, 128], F16)) for i in range(2)]
                wds = e5(nc.sbuf_tensor("wds", [128, D], F32))
                psg = e5(nc.psum_tensor("psg", [128, 1024], F32))
                psu = e5(nc.psum_tensor("psu", [128, 1024], F32))
                psd = e5(nc.psum_tensor("psd", [128, D], F32))

                w_rd = [0, 0]
                w_cast = [0, 0]
                w16_used = [0, 0]
                wd_rd = 0
                wd_cast = 0
                silu_done = 0
                mult_done = 0
                gated_free = 0
                slice_reader = {}
                f_done = {}

                def issue_w(eht):
                    p = eht % 2
                    wt(SY, se, w_cast[p])
                    r = 0
                    for dc in range(16):
                        swgu[p].tick(
                            SY.dma_start(wgs32[p][:, dc, :],
                                         wgT_d[128 * dc:128 * (dc + 1),
                                               128 * eht:128 * (eht + 1)]))
                        r = swgu[p].tick(
                            SY.dma_start(wus32[p][:, dc, :],
                                         wuT_d[128 * dc:128 * (dc + 1),
                                               128 * eht:128 * (eht + 1)]))
                    w_rd[p] = r

                issue_w(0)
                for e8 in range(8):
                    for ht in range(8):
                        eht = 8 * e8 + ht
                        p = eht % 2
                        if eht + 1 < 64:
                            issue_w(eht + 1)
                        wt(SE, ve, wd_cast)
                        wd_rd = swd.tick(
                            SE.dma_start(wds[:], wdT_d[128 * eht:128 * (eht + 1), :]))
                        wt(SE, swgu[p], w_rd[p])
                        wt(SE, te, w16_used[p])
                        SE.copy(wgs16[p][:], wgs32[p][:])
                        w_cast[p] = se.tick(SE.copy(wus16[p][:], wus32[p][:]))
                        wt(TE, se, w_cast[p])
                        wt(TE, se, yT_all)
                        wt(TE, se, silu_done)
                        mg = None
                        for dc in range(16):
                            mg = TE.matmul(psg[:, 0:512], wgs16[p][:, dc, :],
                                           yT[:, dc, 0:512],
                                           start=(dc == 0), stop=(dc == 15))
                            mg = TE.matmul(psg[:, 512:1024], wgs16[p][:, dc, :],
                                           yT[:, dc, 512:1024],
                                           start=(dc == 0), stop=(dc == 15))
                        gate_mm = te.tick(mg)
                        wt(TE, ve, mult_done)
                        mup = None
                        for dc in range(16):
                            mup = TE.matmul(psu[:, 0:512], wus16[p][:, dc, :],
                                            yT[:, dc, 0:512],
                                            start=(dc == 0), stop=(dc == 15))
                            mup = TE.matmul(psu[:, 512:1024], wus16[p][:, dc, :],
                                            yT[:, dc, 512:1024],
                                            start=(dc == 0), stop=(dc == 15))
                        up_mm = te.tick(mup)
                        w16_used[p] = up_mm
                        wt(SE, te, gate_mm)
                        wt(SE, ve, mult_done)
                        silu_done = se.tick(SE.activation(gsil[:], psg[:], ACT.Silu))
                        wt(VE, se, silu_done)
                        wt(VE, te, up_mm)
                        wt(VE, te, gated_free)
                        mult_done = ve.tick(VE.tensor_mul(gatedT[:, ht, :],
                                                          gsil[:], psu[:]))
                        wt(VE, swd, wd_rd)
                        wd_cast = ve.tick(VE.tensor_copy(wd16[:, ht, :], wds[:]))
                    wt(TE, ve, mult_done)
                    wt(TE, ve, wd_cast)
                    for tt in range(NT):
                        for dout in range(4):
                            wt(TE, ve, slice_reader.get(dout, 0))
                            md = None
                            for ht2 in range(8):
                                md = TE.matmul(psd[:, 512 * dout:512 * (dout + 1)],
                                               gatedT[:, ht2, 128 * tt:128 * (tt + 1)],
                                               wd16[:, ht2, 512 * dout:512 * (dout + 1)],
                                               start=(ht2 == 0), stop=(ht2 == 7))
                            dmm = te.tick(md)
                            wt(VE, te, dmm)
                            wt(VE, ve, f_done.get((tt, dout), 0))
                            if e8 == 0:
                                fi = VE.tensor_copy(f[:, tt, 512 * dout:512 * (dout + 1)],
                                                    psd[:, 512 * dout:512 * (dout + 1)])
                            else:
                                fi = VE.tensor_add(f[:, tt, 512 * dout:512 * (dout + 1)],
                                                   f[:, tt, 512 * dout:512 * (dout + 1)],
                                                   psd[:, 512 * dout:512 * (dout + 1)])
                            f_done[(tt, dout)] = ve.tick(fi)
                            slice_reader[dout] = f_done[(tt, dout)]
                    gated_free = te.n
                f_all = max(f_done.values())

            nc.leave_named_scope("P5_ffn", _scid_P5_ffn, False)
            _scid_P6_final = nc.enter_named_scope("P6_final", False)[0]
            for _E in (SY, SE):
                wt(_E, te, te.n)
                wt(_E, ve, ve.n)
                wt(_E, se, se.n)
            with ExitStack() as c5f:
                e5f = c5f.enter_context
                psy = e5f(nc.psum_tensor("psy", [128, D], F16))
                pjk5 = e5f(nc.psum_tensor("pjk5", [128, 128], F16))
                ybk = e5f(nc.sbuf_tensor("ybk", [128, D], F32))
                r2 = e5f(nc.sbuf_tensor("r2", [128, D], F32))
                jnk2 = e5f(nc.sbuf_tensor("jnk2", [128, D], F32))
                ot = [e5f(nc.sbuf_tensor(f"ot{i}", [128, D], F32)) for i in range(2)]
                s1b = e5f(nc.sbuf_tensor("s1b", [128, 1], F32))
                s2b = e5f(nc.sbuf_tensor("s2b", [128, 1], F32))
                mub = e5f(nc.sbuf_tensor("mub", [128, 1], F32))
                m2b = e5f(nc.sbuf_tensor("m2b", [128, 1], F32))
                msqb = e5f(nc.sbuf_tensor("msqb", [128, 1], F32))
                varb = e5f(nc.sbuf_tensor("varb", [128, 1], F32))
                sdb = e5f(nc.sbuf_tensor("sdb", [128, 1], F32))
                rstdb = e5f(nc.sbuf_tensor("rstdb", [128, 1], F32))
                nmrb = e5f(nc.sbuf_tensor("nmrb", [128, 1], F32))

                se_yb_done = 0
                ve_r2_done = 0
                oy = 0
                store_done = {}
                for tt in range(NT):
                    wt(TE, se, se_yb_done)
                    wt(TE, ve, f_all)
                    for cc in range(16):
                        TE.transpose(psy[:, 128 * cc:128 * (cc + 1)],
                                     yT[:, cc, 128 * tt:128 * (tt + 1)], id16[:])
                    yb_done = te.tick(TE.transpose(pjk5[:], id16[:], id16[:]))
                    wt(SE, te, yb_done)
                    wt(SE, ve, ve_r2_done)
                    se_yb_done = se.tick(SE.copy(ybk[:], psy[:]))
                    wt(VE, se, se_yb_done)
                    wt(VE, ve, oy)
                    vch(VE.tensor_add(r2[:], f[:, tt, :], ybk[:]))
                    vch(VE.tensor_scalar(jnk2[:], r2[:], 1.0, 0.0, op0=OP.mult,
                                         op1=OP.add, accum_out=s1b[:]))
                    vch(VE.scalar_tensor_tensor(jnk2[:], r2[:], 1.0, r2[:],
                                                op0=OP.mult, op1=OP.mult, accum_out=s2b[:]))
                    vch(VE.tensor_scalar(mub[:], s1b[:], 1.0 / D, 0.0, op0=OP.mult, op1=OP.add))
                    vch(VE.tensor_scalar(m2b[:], s2b[:], 1.0 / D, 0.0, op0=OP.mult, op1=OP.add))
                    vch(VE.tensor_mul(msqb[:], mub[:], mub[:]))
                    ve_r2_done = ve.tick(VE.tensor_sub(varb[:], m2b[:], msqb[:]))
                    wt(SE, ve, ve_r2_done)
                    sdd = se.tick(SE.activation(sdb[:], varb[:], ACT.Sqrt,
                                                bias=cst[:, 3:4]))
                    wt(VE, se, sdd)
                    vch(VE.reciprocal(rstdb[:], sdb[:]))
                    nmd = vch(VE.tensor_scalar(nmrb[:], mub[:], rstdb[:, 0:1], -1.0,
                                               op0=OP.mult, op1=OP.mult))
                    if tt >= 2:
                        wt(VE, sst[tt % 2], store_done[tt - 2])
                    oy = ve.tick(VE.tensor_scalar(ot[tt % 2][:], r2[:],
                                                  rstdb[:, 0:1], nmrb[:, 0:1],
                                                  op0=OP.mult, op1=OP.add))
                    wt(SE, ve, oy)
                    store_done[tt] = sst[tt % 2].tick(
                        SE.dma_start(out_d[128 * tt:128 * (tt + 1), :], ot[tt % 2][:]))
                wt(SE, sst[0], store_done[NT - 2])
                wt(SE, sst[1], store_done[NT - 1])
            nc.leave_named_scope("P6_final", _scid_P6_final, False)
    return nc


def kernel(L0_attn_x, L0_attn_Wq, L0_attn_Wk, L0_attn_Wv, L0_attn_Wo,
           L0_attn_q_gamma, L0_attn_k_gamma,
           L0_ffn_w_gate, L0_ffn_w_up, L0_ffn_w_down):
    x = np.ascontiguousarray(np.asarray(L0_attn_x, dtype=np.float32))
    wq = np.ascontiguousarray(np.asarray(L0_attn_Wq, dtype=np.float32))
    wk = np.ascontiguousarray(np.asarray(L0_attn_Wk, dtype=np.float32))
    wv = np.ascontiguousarray(np.asarray(L0_attn_Wv, dtype=np.float32))
    wo = np.ascontiguousarray(np.asarray(L0_attn_Wo, dtype=np.float32))
    gq = np.asarray(L0_attn_q_gamma, dtype=np.float32)
    gk = np.asarray(L0_attn_k_gamma, dtype=np.float32)
    wgT = np.ascontiguousarray(np.asarray(L0_ffn_w_gate, dtype=np.float32).T)
    wuT = np.ascontiguousarray(np.asarray(L0_ffn_w_up, dtype=np.float32).T)
    wdT = np.ascontiguousarray(np.asarray(L0_ffn_w_down, dtype=np.float32).T)

    gqb = np.ascontiguousarray(
        np.broadcast_to(np.tile(gq, H), (128, D)).astype(np.float32))
    gkb = np.ascontiguousarray(
        np.broadcast_to(np.tile(gk, KVH), (128, KVH * DH)).astype(np.float32))

    in_maps = []
    metas = []
    for c in range(8):
        b = c % 2
        tpos = [4 * g + c // 2 for g in range(NT)]
        rows = np.concatenate([np.arange(128 * t, 128 * (t + 1)) for t in tpos])
        xq = np.ascontiguousarray(x[b][rows])
        xqT = np.ascontiguousarray(xq.T)
        dmask = np.empty((NT, 128, 512), dtype=np.float32)
        for g in range(NT):
            t = tpos[g]
            qpos = 128 * t + np.arange(128)[:, None]
            kpos = 512 * g + np.arange(512)[None, :]
            dmask[g] = np.where(kpos <= qpos, 3.0e38, -3.0e38)
        in_maps.append({
            "x": x[b], "xq": xq, "xqT": xqT,
            "wq": wq, "wk": wk, "wv": wv, "wo": wo,
            "wgT": wgT, "wuT": wuT, "wdT": wdT,
            "gqb": gqb, "gkb": gkb, "dmask": dmask,
        })
        metas.append((b, tpos))

    nc = build_program()
    res = run_bass_kernel_spmd(nc, in_maps, core_ids=list(range(8)))

    out = np.empty((B, S, D), dtype=np.float32)
    for c in range(8):
        b, tpos = metas[c]
        o = res.results[c]["out"]
        for g in range(NT):
            t = tpos[g]
            out[b, 128 * t:128 * (t + 1), :] = o[128 * g:128 * (g + 1), :]
    return out


# version 05
# speedup vs baseline: 1.1290x; 1.1290x over previous
import numpy as np
from contextlib import ExitStack

import concourse.bass as bass
import concourse.mybir as mybir
from concourse.bass_utils import run_bass_kernel_spmd

F32 = mybir.dt.float32
F16 = mybir.dt.float16
AX = mybir.AxisListType.X
OP = mybir.AluOpType
ACT = mybir.ActivationFunctionType

B, S, D = 2, 4096, 2048
H, KVH, DH = 16, 4, 128
HID = 8192
NT = 8
NKT = S // 128
EPS_RMS = 1e-8
EPS_LN = 1e-5


class Sem:
    def __init__(self, nc, ctx, name, step=1):
        self.h = ctx.enter_context(nc.semaphore(name))
        self.n = 0
        self.step = step

    def tick(self, instr, k=1):
        self.n += k * self.step
        instr.then_inc(self.h, k * self.step)
        return self.n


def build_program():
    nc = bass.Bass("TRN2", target_bir_lowering=False,
                   dynamic_dma_scratch_size=2048)

    x_d = nc.dram_tensor("x", [S, D], F32, kind="ExternalInput").ap()
    xq_d = nc.dram_tensor("xq", [NT * 128, D], F32, kind="ExternalInput").ap()
    xqT_d = nc.dram_tensor("xqT", [D, NT * 128], F32, kind="ExternalInput").ap()
    wq_d = nc.dram_tensor("wq", [D, D], F32, kind="ExternalInput").ap()
    wk_d = nc.dram_tensor("wk", [D, KVH * DH], F32, kind="ExternalInput").ap()
    wv_d = nc.dram_tensor("wv", [D, KVH * DH], F32, kind="ExternalInput").ap()
    wo_d = nc.dram_tensor("wo", [D, D], F32, kind="ExternalInput").ap()
    wgT_d = nc.dram_tensor("wgT", [D, HID], F32, kind="ExternalInput").ap()
    wuT_d = nc.dram_tensor("wuT", [D, HID], F32, kind="ExternalInput").ap()
    wdT_d = nc.dram_tensor("wdT", [HID, D], F32, kind="ExternalInput").ap()
    gqb_d = nc.dram_tensor("gqb", [128, D], F32, kind="ExternalInput").ap()
    gkb_d = nc.dram_tensor("gkb", [128, KVH * DH], F32, kind="ExternalInput").ap()
    dmask_d = nc.dram_tensor("dmask", [NT, 128, 512], F32, kind="ExternalInput").ap()
    out_d = nc.dram_tensor("out", [NT * 128, D], F32, kind="ExternalOutput").ap()

    ident16 = nc.inline_tensor(np.eye(128, dtype=np.float16), "ident16").ap()
    ident32 = nc.inline_tensor(np.eye(128, dtype=np.float32), "ident32").ap()

    TE, VE, SE, SY, GP = nc.tensor, nc.vector, nc.scalar, nc.sync, nc.gpsimd

    wait_memo = {}

    def wt(eng, sem, val):
        if val <= 0:
            return
        key = (id(eng), id(sem))
        if wait_memo.get(key, 0) < val:
            eng.wait_ge(sem.h, val)
            wait_memo[key] = val

    def mk_chains(te, ve, se, TE, VE, SE, wt):
        def vch(instr):
            v = ve.tick(instr)
            wt(VE, ve, v)
            return v

        def sch(instr):
            v = se.tick(instr)
            wt(SE, se, v)
            return v
        return vch, sch

    with ExitStack() as ctx0:
        e0 = ctx0.enter_context
        te = Sem(nc, ctx0, "te")
        ve = Sem(nc, ctx0, "ve")
        se = Sem(nc, ctx0, "se")
        dg = Sem(nc, ctx0, "dg", 16)
        sst = [Sem(nc, ctx0, f"sst{i}", 16) for i in range(2)]
        sxr = [Sem(nc, ctx0, f"sxr{i}", 16) for i in range(2)]
        swkv = [Sem(nc, ctx0, f"swkv{i}", 16) for i in range(2)]
        swq = [Sem(nc, ctx0, f"swq{i}", 16) for i in range(2)]
        sxq = Sem(nc, ctx0, "sxq", 16)
        sdm = [Sem(nc, ctx0, f"sdm{i}", 16) for i in range(2)]
        swo = [Sem(nc, ctx0, f"swo{i}", 16) for i in range(2)]
        sxt = [Sem(nc, ctx0, f"sxt{i}", 16) for i in range(2)]
        swgu = [Sem(nc, ctx0, f"swgu{i}", 16) for i in range(2)]
        swd = Sem(nc, ctx0, "swd", 16)

        vch, sch = mk_chains(te, ve, se, TE, VE, SE, wt)
        id16 = e0(nc.sbuf_tensor("id16", [128, 128], F16))
        id32 = e0(nc.sbuf_tensor("id32", [128, 128], F32))
        cst = e0(nc.sbuf_tensor("cst", [128, 4], F32))
        gqb = e0(nc.sbuf_tensor("gqb_s", [128, D], F16))
        gkb = e0(nc.sbuf_tensor("gkb_s", [128, KVH * DH], F16))

        with ExitStack() as ctxY, ExitStack() as ctxZ, ExitStack() as ctxA:
            yT = ctxY.enter_context(nc.sbuf_tensor("yT", [128, 16, NT * 128], F16))
            zT = ctxZ.enter_context(nc.sbuf_tensor("zT", [128, H, NT * 128], F16))
            eA = ctxA.enter_context
            kT = eA(nc.sbuf_tensor("kT", [128, KVH, S], F16))
            vsb = eA(nc.sbuf_tensor("vsb", [128, NKT, KVH * DH], F16))

            _scid_P1_kv = nc.enter_named_scope("P1_kv", False)[0]
            with ExitStack() as c1:
                e1 = c1.enter_context
                wkv = e1(nc.sbuf_tensor("wkv", [128, 16, 1024], F16))
                wstg = [e1(nc.sbuf_tensor(f"wstg{i}", [128, 1024], F32)) for i in range(2)]
                xr32 = [e1(nc.sbuf_tensor(f"xr32{i}", [128, D], F32)) for i in range(2)]
                xr16 = e1(nc.sbuf_tensor("xr16", [128, D], F16))
                xTt = [e1(nc.sbuf_tensor(f"xTt{i}", [128, 16, 128], F16)) for i in range(2)]
                khat = e1(nc.sbuf_tensor("khat", [128, 512], F16))
                sqk = e1(nc.sbuf_tensor("sqk", [128, 512], F32))
                msk = e1(nc.sbuf_tensor("msk", [128, 4], F32))
                sdk = e1(nc.sbuf_tensor("sdk", [128, 4], F32))
                rk = e1(nc.sbuf_tensor("rk", [128, 4], F32))
                psk = [e1(nc.psum_tensor(f"psk{i}", [128, 512], F32)) for i in range(2)]
                psv = [e1(nc.psum_tensor(f"psv{i}", [128, 512], F32)) for i in range(2)]
                ptk = e1(nc.psum_tensor("ptk", [128, 512], F16))
                ptx = [e1(nc.psum_tensor(f"ptx{i}", [128, 512], F16)) for i in range(2)]
                pjk1 = e1(nc.psum_tensor("pjk1", [128, 128], F16))

                VE.memset(cst[:, 0:1], EPS_RMS)
                VE.memset(cst[:, 1:2], 1.0 / DH)
                VE.memset(cst[:, 2:3], DH * EPS_RMS)
                cs_done = ve.tick(VE.memset(cst[:, 3:4], EPS_LN))

                dg.tick(SY.dma_start(id16[:], ident16))
                dg.tick(SY.dma_start(id32[:], ident32))
                dg.tick(SY.dma_start(wstg[0][:], gqb_d[:, 0:1024]))
                dg.tick(SY.dma_start(wstg[1][:], gqb_d[:, 1024:2048]))
                idr = dg.tick(SY.dma_start(xr32[0][:, 0:512], gkb_d))
                wt(VE, dg, idr)
                VE.tensor_copy(gqb[:, 0:1024], wstg[0][:])
                VE.tensor_copy(gqb[:, 1024:2048], wstg[1][:])
                gam_done = ve.tick(VE.tensor_copy(gkb[:], xr32[0][:, 0:512]))

                wkv_cast = [0, 0]
                wr = [0, 0]

                def issue_wkv(dc):
                    p = dc % 2
                    wt(SY, ve, wkv_cast[p])
                    swkv[p].tick(SY.dma_start(wstg[p][:, 0:512],
                                              wk_d[128 * dc:128 * (dc + 1), :]))
                    wr[p] = swkv[p].tick(
                        SY.dma_start(wstg[p][:, 512:1024],
                                     wv_d[128 * dc:128 * (dc + 1), :]))

                wt(SY, ve, gam_done)
                issue_wkv(0)
                for dc in range(16):
                    p = dc % 2
                    if dc + 1 < 16:
                        issue_wkv(dc + 1)
                    wt(VE, swkv[p], wr[p])
                    wkv_cast[p] = ve.tick(VE.tensor_copy(wkv[:, dc, :], wstg[p][:]))
                wkv_ready = max(wkv_cast)

                xr_ready = [0, 0]
                cast_done = [0, 0]
                xr16_free = 0
                xt_used = [0, 0]
                ptx_ev = [0, 0]
                se_apply_done = [0, 0]
                ve_vcopy_done = [0, 0]
                kt_ev_done = 0

                wt(SE, ve, gam_done)
                xr_ready[0] = sxr[0].tick(SE.dma_start(xr32[0][:], x_d[0:128, :]))
                for kt in range(NKT):
                    p = kt % 2
                    if kt + 1 < NKT:
                        pn = (kt + 1) % 2
                        wt(SE, ve, cast_done[pn])
                        xr_ready[pn] = sxr[pn].tick(
                            SE.dma_start(xr32[pn][:], x_d[128 * (kt + 1):128 * (kt + 2), :]))
                    wt(VE, sxr[p], xr_ready[p])
                    wt(VE, te, xr16_free)
                    cast_done[p] = ve.tick(VE.tensor_copy(xr16[:], xr32[p][:]))
                    wt(TE, ve, cast_done[p])
                    wt(TE, dg, idr)
                    tks = []
                    for grp in range(4):
                        gp_ = grp % 2
                        wt(TE, se, ptx_ev[gp_])
                        tr = None
                        for u in range(4):
                            dc = 4 * grp + u
                            tr = TE.transpose(ptx[gp_][:, 128 * u:128 * (u + 1)],
                                              xr16[:, 128 * dc:128 * (dc + 1)], id16[:])
                        tks.append(te.tick(tr))
                        if grp >= 1:
                            wt(SE, te, tks[grp])
                            wt(SE, te, xt_used[p])
                            gq_ = (grp - 1) % 2
                            ptx_ev[gq_] = se.tick(
                                SE.copy(xTt[p][:, 4 * (grp - 1):4 * grp, :], ptx[gq_][:]))
                    tks.append(te.tick(TE.transpose(pjk1[:], id16[:], id16[:])))
                    wt(SE, te, tks[4])
                    wt(SE, te, xt_used[p])
                    ptx_ev[1] = se.tick(
                        SE.copy(xTt[p][:, 12:16, :], ptx[1][:]))
                    xr16_free = te.n
                    xt_ev = se.n
                    wt(TE, se, se_apply_done[p])
                    wt(TE, ve, ve_vcopy_done[p])
                    wt(TE, ve, wkv_ready)
                    wt(TE, se, xt_ev)
                    mmv = None
                    for dc in range(16):
                        lhsT = xTt[p][:, dc, :]
                        TE.matmul(psk[p][:], lhsT, wkv[:, dc, 0:512],
                                  start=(dc == 0), stop=(dc == 15))
                        mmv = TE.matmul(psv[p][:], lhsT, wkv[:, dc, 512:1024],
                                        start=(dc == 0), stop=(dc == 15))
                    mm_done = te.tick(mmv)
                    xt_used[p] = mm_done
                    wt(SE, te, mm_done)
                    sq_done = se.tick(SE.activation(sqk[:], psk[p][:], ACT.Square))
                    wt(VE, se, sq_done)
                    rs = None
                    for s4 in range(4):
                        rs = VE.reduce_sum(msk[:, s4:s4 + 1],
                                           sqk[:, 128 * s4:128 * (s4 + 1)], axis=AX)
                    red_done = ve.tick(rs)
                    wt(SE, ve, red_done)
                    wt(SE, ve, cs_done)
                    sd_done = se.tick(SE.activation(sdk[:], msk[:], ACT.Sqrt,
                                                    bias=cst[:, 0:1], scale=cst[:, 1:2]))
                    wt(VE, se, sd_done)
                    rk_done = ve.tick(VE.reciprocal(rk[:], sdk[:]))
                    wt(SE, ve, rk_done)
                    ap = None
                    for s4 in range(4):
                        ap = SE.mul(khat[:, 128 * s4:128 * (s4 + 1)],
                                    psk[p][:, 128 * s4:128 * (s4 + 1)], rk[:, s4:s4 + 1])
                    se_apply_done[p] = se.tick(ap)
                    wt(VE, se, se_apply_done[p])
                    kg_done = ve.tick(VE.tensor_mul(khat[:], khat[:], gkb[:]))
                    wt(TE, ve, kg_done)
                    wt(TE, se, kt_ev_done)
                    for s4 in range(4):
                        TE.transpose(ptk[:, 128 * s4:128 * (s4 + 1)],
                                     khat[:, 128 * s4:128 * (s4 + 1)], id16[:])
                    tr_done = te.tick(TE.transpose(pjk1[:], id16[:], id16[:]))
                    wt(SE, te, tr_done)
                    cp = None
                    for s4 in range(4):
                        cp = SE.copy(kT[:, s4, 128 * kt:128 * (kt + 1)],
                                     ptk[:, 128 * s4:128 * (s4 + 1)])
                    kt_ev_done = se.tick(cp)
                    wt(VE, te, mm_done)
                    ve_vcopy_done[p] = ve.tick(VE.tensor_copy(vsb[:, kt, :], psv[p][:]))

            nc.leave_named_scope("P1_kv", _scid_P1_kv, False)
            _scid_P2_q = nc.enter_named_scope("P2_q", False)[0]
            for _E in (SY, SE):
                wt(_E, te, te.n)
                wt(_E, ve, ve.n)
                wt(_E, se, se.n)
            qT = eA(nc.sbuf_tensor("qT", [128, H, NT * 128], F16))
            with ExitStack() as c2:
                e2 = c2.enter_context
                qh16 = e2(nc.sbuf_tensor("qh16", [128, 1024], F16))
                wqs = [e2(nc.sbuf_tensor(f"wqs{i}", [128, 1024], F32)) for i in range(2)]
                xqs = e2(nc.sbuf_tensor("xqs", [128, 16, 256], F32))
                sqq = e2(nc.sbuf_tensor("sqq", [128, 1024], F32))
                msq = e2(nc.sbuf_tensor("msq", [128, 8], F32))
                sdq = e2(nc.sbuf_tensor("sdq", [128, 8], F32))
                rq = e2(nc.sbuf_tensor("rq", [128, 8], F32))
                psq = [e2(nc.psum_tensor(f"psq{i}", [128, 1024], F32)) for i in range(2)]
                ptq = e2(nc.psum_tensor("ptq", [128, 512], F16))
                pjk2 = e2(nc.psum_tensor("pjk2", [128, 128], F16))

                xqs_ready = 0
                xqs_freed = 0
                wq_rd = [0, 0]
                wq_freed = [0, 0]
                se_qapply_done = [0, 0]
                ptq_ev = 0
                mmq_done = {}

                def issue_wq(fh, dc):
                    i = dc % 2
                    wt(SY, te, wq_freed[i])
                    wq_rd[i] = swq[i].tick(
                        SY.dma_start(wqs[i][:],
                                     wq_d[128 * dc:128 * (dc + 1),
                                          1024 * fh:1024 * (fh + 1)]))

                for p4 in range(4):
                    wt(SE, te, xqs_freed)
                    r = 0
                    for dc in range(16):
                        r = sxq.tick(
                            SE.dma_start(xqs[:, dc, :],
                                         xqT_d[128 * dc:128 * (dc + 1),
                                               256 * p4:256 * (p4 + 1)]))
                    xqs_ready = r
                    for fh in range(2):
                        issue_wq(fh, 0)
                        for dc in range(16):
                            if dc + 1 < 16:
                                wq_freed[(dc + 1) % 2] = mmq_done.get(dc - 1, 0)
                                issue_wq(fh, dc + 1)
                            wt(TE, swq[dc % 2], wq_rd[dc % 2])
                            wt(TE, sxq, xqs_ready)
                            mq = None
                            for j in range(2):
                                wt(TE, se, se_qapply_done[j])
                                lhsT = xqs[:, dc, 128 * j:128 * (j + 1)]
                                for nb in range(2):
                                    mq = TE.matmul(psq[j][:, 512 * nb:512 * (nb + 1)],
                                                   lhsT,
                                                   wqs[dc % 2][:, 512 * nb:512 * (nb + 1)],
                                                   start=(dc == 0), stop=(dc == 15))
                            mmq_done[dc] = te.tick(mq)
                        wq_freed = [mmq_done[15], mmq_done[15]]
                        for j in range(2):
                            sl = 2 * p4 + j
                            wt(SE, te, mmq_done[15])
                            sq_done = se.tick(SE.activation(sqq[:], psq[j][:], ACT.Square))
                            wt(VE, se, sq_done)
                            rs = None
                            for s8 in range(8):
                                rs = VE.reduce_sum(msq[:, s8:s8 + 1],
                                                   sqq[:, 128 * s8:128 * (s8 + 1)], axis=AX)
                            red_done = ve.tick(rs)
                            wt(SE, ve, red_done)
                            sd_done = se.tick(SE.activation(sdq[:], msq[:], ACT.Sqrt,
                                                            bias=cst[:, 2:3], scale=1.0))
                            wt(VE, se, sd_done)
                            rq_done = ve.tick(VE.reciprocal(rq[:], sdq[:]))
                            wt(SE, ve, rq_done)
                            ap = None
                            for s8 in range(8):
                                ap = SE.mul(qh16[:, 128 * s8:128 * (s8 + 1)],
                                            psq[j][:, 128 * s8:128 * (s8 + 1)],
                                            rq[:, s8:s8 + 1])
                            se_qapply_done[j] = se.tick(ap)
                            wt(VE, se, se_qapply_done[j])
                            qg_done = ve.tick(VE.tensor_mul(
                                qh16[:], qh16[:], gqb[:, 1024 * fh:1024 * (fh + 1)]))
                            wt(TE, ve, qg_done)
                            for grp in range(2):
                                wt(TE, se, ptq_ev)
                                tr = None
                                for u in range(4):
                                    s8 = 4 * grp + u
                                    tr = TE.transpose(ptq[:, 128 * u:128 * (u + 1)],
                                                      qh16[:, 128 * s8:128 * (s8 + 1)],
                                                      id16[:])
                                te.tick(tr)
                                trd = te.tick(TE.transpose(pjk2[:], id16[:], id16[:]))
                                wt(SE, te, trd)
                                cp = None
                                for u in range(4):
                                    hh = 8 * fh + 4 * grp + u
                                    cp = SE.copy(qT[:, hh, 128 * sl:128 * (sl + 1)],
                                                 ptq[:, 128 * u:128 * (u + 1)])
                                ptq_ev = se.tick(cp)
                    xqs_freed = te.n

            nc.leave_named_scope("P2_q", _scid_P2_q, False)
            _scid_P3_attn = nc.enter_named_scope("P3_attn", False)[0]
            for _E in (SY, SE):
                wt(_E, te, te.n)
                wt(_E, ve, ve.n)
                wt(_E, se, se.n)
            p23_se = se.n
            p23_ve = ve.n
            wt(TE, se, p23_se)
            wt(TE, ve, p23_ve)
            with ExitStack() as c3:
                e3 = c3.enter_context
                scr = [e3(nc.sbuf_tensor(f"scr{i}", [128, 4096], F32)) for i in range(2)]
                ph = [e3(nc.sbuf_tensor(f"ph{i}", [128, 4096], F16)) for i in range(2)]
                pt4 = [e3(nc.sbuf_tensor(f"pt4{i}", [128, 512], F16)) for i in range(2)]
                dmb = [e3(nc.sbuf_tensor(f"dmb{i}", [128, 512], F32)) for i in range(2)]
                nm = [e3(nc.sbuf_tensor(f"nm{i}", [128, 1], F32)) for i in range(2)]
                ell = [e3(nc.sbuf_tensor(f"ell{i}", [128, 1], F32)) for i in range(2)]
                li = [e3(nc.sbuf_tensor(f"li{i}", [128, 1], F32)) for i in range(2)]
                pss = [e3(nc.psum_tensor(f"pss{i}", [128, 512], F32)) for i in range(2)]
                pst = [e3(nc.psum_tensor(f"pst{i}", [128, 512], F16)) for i in range(2)]
                pso = [e3(nc.psum_tensor(f"pso{i}", [128, 128], F32)) for i in range(2)]
                pjk3 = e3(nc.psum_tensor("pjk3", [128, 128], F16))

                NIT = NT * H
                def it_gh(i):
                    return i // H, i % H

                sc = [0, 0]
                ve_scr_done = [0, 0]
                nm_done = [0, 0]
                ex_done = [0, 0]
                te_ph_done = [0, 0]
                se_pt_done = [0, 0]
                zev_done = [0, 0]
                dm_ready = [0, 0]
                dm_used = [0, 0]
                dm_ready[0] = sdm[0].tick(SE.dma_start(dmb[0][:], dmask_d[0]))
                dm_issued = 0

                def emit_stats(i):
                    g, h = it_gh(i)
                    nb = g + 1
                    rho = i % 2
                    g4 = h // 4
                    for kb in range(nb):
                        p = kb % 2
                        wt(TE, se, sc[p])
                        mm = TE.matmul(pss[p][:], qT[:, h, 128 * g:128 * (g + 1)],
                                       kT[:, g4, 512 * kb:512 * (kb + 1)],
                                       start=True, stop=True)
                        mmd = te.tick(mm)
                        wt(SE, te, mmd)
                        wt(SE, ve, ve_scr_done[rho])
                        sc[p] = se.tick(SE.copy(scr[rho][:, 512 * kb:512 * (kb + 1)],
                                                pss[p][:]))

                def emit_sm_front(i):
                    g, h = it_gh(i)
                    nb = g + 1
                    rho = i % 2
                    wt(VE, se, max(sc))
                    wt(VE, sdm[g % 2], dm_ready[g % 2])
                    vch(VE.tensor_tensor(scr[rho][:, 512 * g:512 * (g + 1)],
                                         scr[rho][:, 512 * g:512 * (g + 1)],
                                         dmb[g % 2][:], op=OP.min))
                    nm_done[rho] = vch(VE.tensor_reduce(nm[rho][:],
                                                        scr[rho][:, 0:512 * nb],
                                                        axis=AX, op=OP.max, negate=True))
                    ve_scr_done[rho] = nm_done[rho]
                    dm_used[g % 2] = nm_done[rho]

                def emit_sm_back(i):
                    g, h = it_gh(i)
                    nb = g + 1
                    rho = i % 2
                    wt(SE, ve, nm_done[rho])
                    wt(SE, ve, li_done[rho])
                    wt(SE, te, te_ph_done[rho])
                    exp_done = sch(SE.activation(ph[rho][:, 0:512 * nb],
                                                 scr[rho][:, 0:512 * nb],
                                                 ACT.Exp, bias=nm[rho][:],
                                                 accum_out=ell[rho][:]))
                    wt(VE, se, exp_done)
                    li_done[rho] = vch(VE.reciprocal(li[rho][:], ell[rho][:]))
                    ex_done[rho] = vch(VE.tensor_scalar(ph[rho][:, 0:512 * nb],
                                                        ph[rho][:, 0:512 * nb],
                                                        li[rho][:, 0:1], 0.0,
                                                        op0=OP.mult, op1=OP.add))

                def emit_tav(i):
                    g, h = it_gh(i)
                    nb = g + 1
                    rho = i % 2
                    g4 = h // 4
                    EV = SE if (i % 2 == 0) else VE
                    evs = se if (i % 2 == 0) else ve
                    wt(TE, ve, ex_done[rho])
                    wt(TE, se if zev_eng[rho] == 0 else ve, zev_done[rho])

                    def emit_tchunk(q4):
                        pp = q4 % 2
                        wt(TE, se, se_pt_done[pp] if se_pt_eng[pp] == 0 else 0)
                        wt(TE, ve, se_pt_done[pp] if se_pt_eng[pp] == 1 else 0)
                        tr = None
                        for u in range(4):
                            tr = TE.transpose(pst[pp][:, 128 * u:128 * (u + 1)],
                                              ph[rho][:, 512 * q4 + 128 * u:
                                                       512 * q4 + 128 * (u + 1)], id16[:])
                        return te.tick(tr)

                    tch = [0] * (nb + 1)
                    tch[0] = emit_tchunk(0)
                    av_done = 0
                    for q4 in range(nb):
                        if q4 + 1 < nb:
                            tch[q4 + 1] = emit_tchunk(q4 + 1)
                        else:
                            tch[q4 + 1] = te.tick(
                                TE.transpose(pjk3[:], id16[:], id16[:]))
                        wt(EV, te, tch[q4 + 1])
                        if EV is SE:
                            se_pt_done[q4 % 2] = evs.tick(
                                SE.copy(pt4[q4 % 2][:], pst[q4 % 2][:]))
                            se_pt_eng[q4 % 2] = 0
                        else:
                            se_pt_done[q4 % 2] = evs.tick(
                                VE.tensor_copy(pt4[q4 % 2][:], pst[q4 % 2][:]))
                            se_pt_eng[q4 % 2] = 1
                        wt(TE, se if se_pt_eng[q4 % 2] == 0 else ve,
                           se_pt_done[q4 % 2])
                        av = None
                        for u in range(4):
                            ktile = 4 * q4 + u
                            av = TE.matmul(pso[rho][:],
                                           vsb[:, ktile, 128 * g4:128 * (g4 + 1)],
                                           pt4[q4 % 2][:, 128 * u:128 * (u + 1)],
                                           start=(ktile == 0),
                                           stop=(ktile == 4 * nb - 1))
                        av_done = te.tick(av)
                    te_ph_done[rho] = te.n
                    wt(EV, te, av_done)
                    if EV is SE:
                        zev_done[rho] = evs.tick(
                            SE.copy(zT[:, h, 128 * g:128 * (g + 1)], pso[rho][:]))
                        zev_eng[rho] = 0
                    else:
                        zev_done[rho] = evs.tick(
                            VE.tensor_copy(zT[:, h, 128 * g:128 * (g + 1)], pso[rho][:]))
                        zev_eng[rho] = 1

                se_pt_eng = [0, 0]
                zev_eng = [0, 0]
                li_done = [0, 0]
                emit_stats(0)
                emit_stats(1)
                emit_sm_front(0)
                for i in range(NIT):
                    g, h = it_gh(i)
                    if h == 0 and g + 1 < NT and dm_issued < g + 1:
                        pn = (g + 1) % 2
                        wt(SE, ve, dm_used[pn])
                        dm_ready[pn] = sdm[pn].tick(
                            SE.dma_start(dmb[pn][:], dmask_d[g + 1]))
                        dm_issued = g + 1
                    if i + 1 < NIT:
                        emit_sm_front(i + 1)
                    emit_sm_back(i)
                    if i + 2 < NIT:
                        emit_stats(i + 2)
                    emit_tav(i)
            zev_all = se.n
            ctxA.close()

            nc.leave_named_scope("P3_attn", _scid_P3_attn, False)
            _scid_P4_wo = nc.enter_named_scope("P4_wo", False)[0]
            for _E in (SY, SE):
                wt(_E, te, te.n)
                wt(_E, ve, ve.n)
                wt(_E, se, se.n)
            with ExitStack() as c4:
                e4 = c4.enter_context
                wo16 = e4(nc.sbuf_tensor("wo16", [128, 16, D], F16))
                wos = [e4(nc.sbuf_tensor(f"wos{i}", [128, D], F32)) for i in range(2)]
                xts = [e4(nc.sbuf_tensor(f"xts{i}", [128, D], F32)) for i in range(2)]
                rbuf = e4(nc.sbuf_tensor("rbuf", [128, D], F32))
                jnk = e4(nc.sbuf_tensor("jnk", [128, D], F32))
                ytile = e4(nc.sbuf_tensor("ytile", [128, D], F16))
                s1 = e4(nc.sbuf_tensor("s1", [128, 1], F32))
                s2 = e4(nc.sbuf_tensor("s2", [128, 1], F32))
                mu = e4(nc.sbuf_tensor("mu", [128, 1], F32))
                m2m = e4(nc.sbuf_tensor("m2m", [128, 1], F32))
                msq2 = e4(nc.sbuf_tensor("msq2", [128, 1], F32))
                var = e4(nc.sbuf_tensor("var", [128, 1], F32))
                sd = e4(nc.sbuf_tensor("sd", [128, 1], F32))
                rstd = e4(nc.sbuf_tensor("rstd", [128, 1], F32))
                nmr = e4(nc.sbuf_tensor("nmr", [128, 1], F32))
                pswo = e4(nc.psum_tensor("pswo", [128, D], F32))
                psyt = [e4(nc.psum_tensor(f"psyt{i}", [128, 512], F16)) for i in range(2)]
                pjk4 = e4(nc.psum_tensor("pjk4", [128, 128], F16))

                wo_cast = [0, 0]
                wo_rd = [0, 0]

                def issue_wo(dc):
                    p = dc % 2
                    wt(SY, ve, wo_cast[p])
                    wo_rd[p] = swo[p].tick(
                        SY.dma_start(wos[p][:], wo_d[128 * dc:128 * (dc + 1), :]))

                issue_wo(0)
                for dc in range(16):
                    p = dc % 2
                    if dc + 1 < 16:
                        issue_wo(dc + 1)
                    wt(VE, swo[p], wo_rd[p])
                    wo_cast[p] = ve.tick(VE.tensor_copy(wo16[:, dc, :], wos[p][:]))
                wo_all = max(wo_cast)

                xq_ready = [0, 0]
                xq_ready[0] = sxt[0].tick(SE.dma_start(xts[0][:], xq_d[0:128, :]))

                ve_r_done = 0
                yt_te_done = 0
                yt_ev = [0, 0]
                for g in range(NT):
                    if g + 1 < NT:
                        pn = (g + 1) % 2
                        wt(SE, ve, ve_r_done)
                        xq_ready[pn] = sxt[pn].tick(
                            SE.dma_start(xts[pn][:], xq_d[128 * (g + 1):128 * (g + 2), :]))
                    wt(TE, ve, wo_all)
                    wt(TE, se, zev_all)
                    wt(TE, ve, ve_r_done)
                    mo = None
                    for hh in range(16):
                        for nbk in range(4):
                            mo = TE.matmul(pswo[:, 512 * nbk:512 * (nbk + 1)],
                                           zT[:, hh, 128 * g:128 * (g + 1)],
                                           wo16[:, hh, 512 * nbk:512 * (nbk + 1)],
                                           start=(hh == 0), stop=(hh == 15))
                    wo_mm = te.tick(mo)
                    wt(VE, te, wo_mm)
                    wt(VE, sxt[g % 2], xq_ready[g % 2])
                    vch(VE.scalar_tensor_tensor(rbuf[:], xts[g % 2][:], 2.0, pswo[:],
                                                op0=OP.mult, op1=OP.add))
                    vch(VE.tensor_scalar(jnk[:], rbuf[:], 1.0, 0.0, op0=OP.mult,
                                         op1=OP.add, accum_out=s1[:]))
                    vch(VE.scalar_tensor_tensor(jnk[:], rbuf[:], 1.0, rbuf[:],
                                                op0=OP.mult, op1=OP.mult, accum_out=s2[:]))
                    vch(VE.tensor_scalar(mu[:], s1[:], 1.0 / D, 0.0, op0=OP.mult, op1=OP.add))
                    vch(VE.tensor_scalar(m2m[:], s2[:], 1.0 / D, 0.0, op0=OP.mult, op1=OP.add))
                    vch(VE.tensor_mul(msq2[:], mu[:], mu[:]))
                    ve_r_done = ve.tick(VE.tensor_sub(var[:], m2m[:], msq2[:]))
                    wt(SE, ve, ve_r_done)
                    sd_done = se.tick(SE.activation(sd[:], var[:], ACT.Sqrt,
                                                    bias=cst[:, 3:4]))
                    wt(VE, se, sd_done)
                    vch(VE.reciprocal(rstd[:], sd[:]))
                    nmr_done = vch(VE.tensor_scalar(nmr[:], mu[:], rstd[:, 0:1], -1.0,
                                                    op0=OP.mult, op1=OP.mult))
                    wt(VE, te, yt_te_done)
                    ve_y_done = ve.tick(VE.tensor_scalar(ytile[:], rbuf[:],
                                                         rstd[:, 0:1], nmr[:, 0:1],
                                                         op0=OP.mult, op1=OP.add))
                    wt(TE, ve, ve_y_done)
                    ytk = []
                    for grp in range(4):
                        pp = grp % 2
                        wt(TE, se, yt_ev[pp])
                        tr = None
                        for u in range(4):
                            cc = 4 * grp + u
                            tr = TE.transpose(psyt[pp][:, 128 * u:128 * (u + 1)],
                                              ytile[:, 128 * cc:128 * (cc + 1)], id16[:])
                        ytk.append(te.tick(tr))
                        if grp >= 1:
                            wt(SE, te, ytk[grp])
                            cp = None
                            for u2 in range(4):
                                c0 = 4 * (grp - 1) + u2
                                cp = SE.copy(yT[:, c0, 128 * g:128 * (g + 1)],
                                             psyt[(grp - 1) % 2][:, 128 * u2:128 * (u2 + 1)])
                            yt_ev[(grp - 1) % 2] = se.tick(cp)
                    ytk.append(te.tick(TE.transpose(pjk4[:], id16[:], id16[:])))
                    wt(SE, te, ytk[4])
                    cp = None
                    for u2 in range(4):
                        c0 = 12 + u2
                        cp = SE.copy(yT[:, c0, 128 * g:128 * (g + 1)],
                                     psyt[1][:, 128 * u2:128 * (u2 + 1)])
                    yt_ev[1] = se.tick(cp)
                    yt_te_done = te.n
                yT_all = se.n
            ctxZ.close()

            nc.leave_named_scope("P4_wo", _scid_P4_wo, False)
            _scid_P5_ffn = nc.enter_named_scope("P5_ffn", False)[0]
            for _E in (SY, SE):
                wt(_E, te, te.n)
                wt(_E, ve, ve.n)
                wt(_E, se, se.n)
            f = ctxY.enter_context(nc.sbuf_tensor("f", [128, NT, D], F32))
            with ExitStack() as c5m:
                e5 = c5m.enter_context
                gatedT = e5(nc.sbuf_tensor("gatedT", [128, 8, 1024], F16))
                wd16 = e5(nc.sbuf_tensor("wd16", [128, 8, D], F16))
                gsil = e5(nc.sbuf_tensor("gsil", [128, 1024], F16))
                wgs32 = [e5(nc.sbuf_tensor(f"wgs32{i}", [128, 16, 128], F32)) for i in range(2)]
                wus32 = [e5(nc.sbuf_tensor(f"wus32{i}", [128, 16, 128], F32)) for i in range(2)]
                wgs16 = [e5(nc.sbuf_tensor(f"wgs16{i}", [128, 16, 128], F16)) for i in range(2)]
                wus16 = [e5(nc.sbuf_tensor(f"wus16{i}", [128, 16, 128], F16)) for i in range(2)]
                wds = e5(nc.sbuf_tensor("wds", [128, D], F32))
                psg = e5(nc.psum_tensor("psg", [128, 1024], F32))
                psu = e5(nc.psum_tensor("psu", [128, 1024], F32))
                psd = e5(nc.psum_tensor("psd", [128, D], F32))

                w_rd = [0, 0]
                w_cast = [0, 0]
                w16_used = [0, 0]
                wd_rd = 0
                wd_cast = 0
                silu_done = 0
                mult_done = 0
                gated_free = 0
                slice_reader = {}
                f_done = {}

                def issue_w(eht):
                    p = eht % 2
                    wt(SY, se, w_cast[p])
                    r = 0
                    for dc in range(16):
                        swgu[p].tick(
                            SY.dma_start(wgs32[p][:, dc, :],
                                         wgT_d[128 * dc:128 * (dc + 1),
                                               128 * eht:128 * (eht + 1)]))
                        r = swgu[p].tick(
                            SY.dma_start(wus32[p][:, dc, :],
                                         wuT_d[128 * dc:128 * (dc + 1),
                                               128 * eht:128 * (eht + 1)]))
                    w_rd[p] = r

                issue_w(0)
                for e8 in range(8):
                    for ht in range(8):
                        eht = 8 * e8 + ht
                        p = eht % 2
                        if eht + 1 < 64:
                            issue_w(eht + 1)
                        wt(SE, ve, wd_cast)
                        wd_rd = swd.tick(
                            SE.dma_start(wds[:], wdT_d[128 * eht:128 * (eht + 1), :]))
                        wt(SE, swgu[p], w_rd[p])
                        wt(SE, te, w16_used[p])
                        SE.copy(wgs16[p][:], wgs32[p][:])
                        w_cast[p] = se.tick(SE.copy(wus16[p][:], wus32[p][:]))
                        wt(TE, se, w_cast[p])
                        wt(TE, se, yT_all)
                        wt(TE, se, silu_done)
                        mg = None
                        for dc in range(16):
                            mg = TE.matmul(psg[:, 0:512], wgs16[p][:, dc, :],
                                           yT[:, dc, 0:512],
                                           start=(dc == 0), stop=(dc == 15))
                            mg = TE.matmul(psg[:, 512:1024], wgs16[p][:, dc, :],
                                           yT[:, dc, 512:1024],
                                           start=(dc == 0), stop=(dc == 15))
                        gate_mm = te.tick(mg)
                        wt(TE, ve, mult_done)
                        mup = None
                        for dc in range(16):
                            mup = TE.matmul(psu[:, 0:512], wus16[p][:, dc, :],
                                            yT[:, dc, 0:512],
                                            start=(dc == 0), stop=(dc == 15))
                            mup = TE.matmul(psu[:, 512:1024], wus16[p][:, dc, :],
                                            yT[:, dc, 512:1024],
                                            start=(dc == 0), stop=(dc == 15))
                        up_mm = te.tick(mup)
                        w16_used[p] = up_mm
                        wt(SE, te, gate_mm)
                        wt(SE, ve, mult_done)
                        silu_done = se.tick(SE.activation(gsil[:], psg[:], ACT.Silu))
                        wt(VE, se, silu_done)
                        wt(VE, te, up_mm)
                        wt(VE, te, gated_free)
                        mult_done = ve.tick(VE.tensor_mul(gatedT[:, ht, :],
                                                          gsil[:], psu[:]))
                        wt(VE, swd, wd_rd)
                        wd_cast = ve.tick(VE.tensor_copy(wd16[:, ht, :], wds[:]))
                    wt(TE, ve, mult_done)
                    wt(TE, ve, wd_cast)
                    for tt in range(NT):
                        for dout in range(4):
                            wt(TE, ve, slice_reader.get(dout, 0))
                            md = None
                            for ht2 in range(8):
                                md = TE.matmul(psd[:, 512 * dout:512 * (dout + 1)],
                                               gatedT[:, ht2, 128 * tt:128 * (tt + 1)],
                                               wd16[:, ht2, 512 * dout:512 * (dout + 1)],
                                               start=(ht2 == 0), stop=(ht2 == 7))
                            dmm = te.tick(md)
                            wt(VE, te, dmm)
                            wt(VE, ve, f_done.get((tt, dout), 0))
                            if e8 == 0:
                                fi = VE.tensor_copy(f[:, tt, 512 * dout:512 * (dout + 1)],
                                                    psd[:, 512 * dout:512 * (dout + 1)])
                            else:
                                fi = VE.tensor_add(f[:, tt, 512 * dout:512 * (dout + 1)],
                                                   f[:, tt, 512 * dout:512 * (dout + 1)],
                                                   psd[:, 512 * dout:512 * (dout + 1)])
                            f_done[(tt, dout)] = ve.tick(fi)
                            slice_reader[dout] = f_done[(tt, dout)]
                    gated_free = te.n
                f_all = max(f_done.values())

            nc.leave_named_scope("P5_ffn", _scid_P5_ffn, False)
            _scid_P6_final = nc.enter_named_scope("P6_final", False)[0]
            for _E in (SY, SE):
                wt(_E, te, te.n)
                wt(_E, ve, ve.n)
                wt(_E, se, se.n)
            with ExitStack() as c5f:
                e5f = c5f.enter_context
                psy = e5f(nc.psum_tensor("psy", [128, D], F16))
                pjk5 = e5f(nc.psum_tensor("pjk5", [128, 128], F16))
                ybk = e5f(nc.sbuf_tensor("ybk", [128, D], F32))
                r2 = e5f(nc.sbuf_tensor("r2", [128, D], F32))
                jnk2 = e5f(nc.sbuf_tensor("jnk2", [128, D], F32))
                ot = [e5f(nc.sbuf_tensor(f"ot{i}", [128, D], F32)) for i in range(2)]
                s1b = e5f(nc.sbuf_tensor("s1b", [128, 1], F32))
                s2b = e5f(nc.sbuf_tensor("s2b", [128, 1], F32))
                mub = e5f(nc.sbuf_tensor("mub", [128, 1], F32))
                m2b = e5f(nc.sbuf_tensor("m2b", [128, 1], F32))
                msqb = e5f(nc.sbuf_tensor("msqb", [128, 1], F32))
                varb = e5f(nc.sbuf_tensor("varb", [128, 1], F32))
                sdb = e5f(nc.sbuf_tensor("sdb", [128, 1], F32))
                rstdb = e5f(nc.sbuf_tensor("rstdb", [128, 1], F32))
                nmrb = e5f(nc.sbuf_tensor("nmrb", [128, 1], F32))

                se_yb_done = 0
                ve_r2_done = 0
                oy = 0
                store_done = {}
                for tt in range(NT):
                    wt(TE, se, se_yb_done)
                    wt(TE, ve, f_all)
                    for cc in range(16):
                        TE.transpose(psy[:, 128 * cc:128 * (cc + 1)],
                                     yT[:, cc, 128 * tt:128 * (tt + 1)], id16[:])
                    yb_done = te.tick(TE.transpose(pjk5[:], id16[:], id16[:]))
                    wt(SE, te, yb_done)
                    wt(SE, ve, ve_r2_done)
                    se_yb_done = se.tick(SE.copy(ybk[:], psy[:]))
                    wt(VE, se, se_yb_done)
                    wt(VE, ve, oy)
                    vch(VE.tensor_add(r2[:], f[:, tt, :], ybk[:]))
                    vch(VE.tensor_scalar(jnk2[:], r2[:], 1.0, 0.0, op0=OP.mult,
                                         op1=OP.add, accum_out=s1b[:]))
                    vch(VE.scalar_tensor_tensor(jnk2[:], r2[:], 1.0, r2[:],
                                                op0=OP.mult, op1=OP.mult, accum_out=s2b[:]))
                    vch(VE.tensor_scalar(mub[:], s1b[:], 1.0 / D, 0.0, op0=OP.mult, op1=OP.add))
                    vch(VE.tensor_scalar(m2b[:], s2b[:], 1.0 / D, 0.0, op0=OP.mult, op1=OP.add))
                    vch(VE.tensor_mul(msqb[:], mub[:], mub[:]))
                    ve_r2_done = ve.tick(VE.tensor_sub(varb[:], m2b[:], msqb[:]))
                    wt(SE, ve, ve_r2_done)
                    sdd = se.tick(SE.activation(sdb[:], varb[:], ACT.Sqrt,
                                                bias=cst[:, 3:4]))
                    wt(VE, se, sdd)
                    vch(VE.reciprocal(rstdb[:], sdb[:]))
                    nmd = vch(VE.tensor_scalar(nmrb[:], mub[:], rstdb[:, 0:1], -1.0,
                                               op0=OP.mult, op1=OP.mult))
                    if tt >= 2:
                        wt(VE, sst[tt % 2], store_done[tt - 2])
                    oy = ve.tick(VE.tensor_scalar(ot[tt % 2][:], r2[:],
                                                  rstdb[:, 0:1], nmrb[:, 0:1],
                                                  op0=OP.mult, op1=OP.add))
                    wt(SE, ve, oy)
                    store_done[tt] = sst[tt % 2].tick(
                        SE.dma_start(out_d[128 * tt:128 * (tt + 1), :], ot[tt % 2][:]))
                wt(SE, sst[0], store_done[NT - 2])
                wt(SE, sst[1], store_done[NT - 1])
            nc.leave_named_scope("P6_final", _scid_P6_final, False)
    return nc


def kernel(L0_attn_x, L0_attn_Wq, L0_attn_Wk, L0_attn_Wv, L0_attn_Wo,
           L0_attn_q_gamma, L0_attn_k_gamma,
           L0_ffn_w_gate, L0_ffn_w_up, L0_ffn_w_down):
    x = np.ascontiguousarray(np.asarray(L0_attn_x, dtype=np.float32))
    wq = np.ascontiguousarray(np.asarray(L0_attn_Wq, dtype=np.float32))
    wk = np.ascontiguousarray(np.asarray(L0_attn_Wk, dtype=np.float32))
    wv = np.ascontiguousarray(np.asarray(L0_attn_Wv, dtype=np.float32))
    wo = np.ascontiguousarray(np.asarray(L0_attn_Wo, dtype=np.float32))
    gq = np.asarray(L0_attn_q_gamma, dtype=np.float32)
    gk = np.asarray(L0_attn_k_gamma, dtype=np.float32)
    wgT = np.ascontiguousarray(np.asarray(L0_ffn_w_gate, dtype=np.float32).T)
    wuT = np.ascontiguousarray(np.asarray(L0_ffn_w_up, dtype=np.float32).T)
    wdT = np.ascontiguousarray(np.asarray(L0_ffn_w_down, dtype=np.float32).T)

    gqb = np.ascontiguousarray(
        np.broadcast_to(np.tile(gq, H), (128, D)).astype(np.float32))
    gkb = np.ascontiguousarray(
        np.broadcast_to(np.tile(gk, KVH), (128, KVH * DH)).astype(np.float32))

    in_maps = []
    metas = []
    for c in range(8):
        b = c % 2
        tpos = [4 * g + c // 2 for g in range(NT)]
        rows = np.concatenate([np.arange(128 * t, 128 * (t + 1)) for t in tpos])
        xq = np.ascontiguousarray(x[b][rows])
        xqT = np.ascontiguousarray(xq.T)
        dmask = np.empty((NT, 128, 512), dtype=np.float32)
        for g in range(NT):
            t = tpos[g]
            qpos = 128 * t + np.arange(128)[:, None]
            kpos = 512 * g + np.arange(512)[None, :]
            dmask[g] = np.where(kpos <= qpos, 3.0e38, -3.0e38)
        in_maps.append({
            "x": x[b], "xq": xq, "xqT": xqT,
            "wq": wq, "wk": wk, "wv": wv, "wo": wo,
            "wgT": wgT, "wuT": wuT, "wdT": wdT,
            "gqb": gqb, "gkb": gkb, "dmask": dmask,
        })
        metas.append((b, tpos))

    nc = build_program()
    res = run_bass_kernel_spmd(nc, in_maps, core_ids=list(range(8)))

    out = np.empty((B, S, D), dtype=np.float32)
    for c in range(8):
        b, tpos = metas[c]
        o = res.results[c]["out"]
        for g in range(NT):
            t = tpos[g]
            out[b, 128 * t:128 * (t + 1), :] = o[128 * g:128 * (g + 1), :]
    return out


# version 06
# speedup vs baseline: 1.1659x; 1.0327x over previous
import numpy as np
from contextlib import ExitStack

import concourse.bass as bass
import concourse.mybir as mybir
from concourse.bass_utils import run_bass_kernel_spmd

F32 = mybir.dt.float32
F16 = mybir.dt.float16
AX = mybir.AxisListType.X
OP = mybir.AluOpType
ACT = mybir.ActivationFunctionType

B, S, D = 2, 4096, 2048
H, KVH, DH = 16, 4, 128
HID = 8192
NT = 8
NKT = S // 128
EPS_RMS = 1e-8
EPS_LN = 1e-5


class Sem:
    def __init__(self, nc, ctx, name, step=1):
        self.h = ctx.enter_context(nc.semaphore(name))
        self.n = 0
        self.step = step

    def tick(self, instr, k=1):
        self.n += k * self.step
        instr.then_inc(self.h, k * self.step)
        return self.n


def build_program():
    nc = bass.Bass("TRN2", target_bir_lowering=False,
                   dynamic_dma_scratch_size=2048)

    x_d = nc.dram_tensor("x", [S, D], F32, kind="ExternalInput").ap()
    xq_d = nc.dram_tensor("xq", [NT * 128, D], F32, kind="ExternalInput").ap()
    xqT_d = nc.dram_tensor("xqT", [D, NT * 128], F32, kind="ExternalInput").ap()
    wq_d = nc.dram_tensor("wq", [D, D], F32, kind="ExternalInput").ap()
    wk_d = nc.dram_tensor("wk", [D, KVH * DH], F32, kind="ExternalInput").ap()
    wv_d = nc.dram_tensor("wv", [D, KVH * DH], F32, kind="ExternalInput").ap()
    wo_d = nc.dram_tensor("wo", [D, D], F32, kind="ExternalInput").ap()
    wgT_d = nc.dram_tensor("wgT", [D, HID], F32, kind="ExternalInput").ap()
    wuT_d = nc.dram_tensor("wuT", [D, HID], F32, kind="ExternalInput").ap()
    wdT_d = nc.dram_tensor("wdT", [HID, D], F32, kind="ExternalInput").ap()
    gqb_d = nc.dram_tensor("gqb", [128, D], F32, kind="ExternalInput").ap()
    gkb_d = nc.dram_tensor("gkb", [128, KVH * DH], F32, kind="ExternalInput").ap()
    dmask_d = nc.dram_tensor("dmask", [NT, 128, 512], F32, kind="ExternalInput").ap()
    out_d = nc.dram_tensor("out", [NT * 128, D], F32, kind="ExternalOutput").ap()

    ident16 = nc.inline_tensor(np.eye(128, dtype=np.float16), "ident16").ap()
    ident32 = nc.inline_tensor(np.eye(128, dtype=np.float32), "ident32").ap()

    TE, VE, SE, SY, GP = nc.tensor, nc.vector, nc.scalar, nc.sync, nc.gpsimd

    wait_memo = {}

    def wt(eng, sem, val):
        if val <= 0:
            return
        key = (id(eng), id(sem))
        if wait_memo.get(key, 0) < val:
            eng.wait_ge(sem.h, val)
            wait_memo[key] = val

    def mk_chains(te, ve, se, TE, VE, SE, wt):
        def vch(instr):
            v = ve.tick(instr)
            wt(VE, ve, v)
            return v

        def sch(instr):
            v = se.tick(instr)
            wt(SE, se, v)
            return v
        return vch, sch

    with ExitStack() as ctx0:
        e0 = ctx0.enter_context
        te = Sem(nc, ctx0, "te")
        ve = Sem(nc, ctx0, "ve")
        se = Sem(nc, ctx0, "se")
        dg = Sem(nc, ctx0, "dg", 16)
        sst = [Sem(nc, ctx0, f"sst{i}", 16) for i in range(2)]
        sxr = [Sem(nc, ctx0, f"sxr{i}", 16) for i in range(2)]
        swkv = [Sem(nc, ctx0, f"swkv{i}", 16) for i in range(2)]
        swq = [Sem(nc, ctx0, f"swq{i}", 16) for i in range(2)]
        sxq = Sem(nc, ctx0, "sxq", 16)
        sdm = [Sem(nc, ctx0, f"sdm{i}", 16) for i in range(2)]
        swo = [Sem(nc, ctx0, f"swo{i}", 16) for i in range(2)]
        sxt = [Sem(nc, ctx0, f"sxt{i}", 16) for i in range(2)]
        swgu = [Sem(nc, ctx0, f"swgu{i}", 16) for i in range(2)]
        swd = Sem(nc, ctx0, "swd", 16)

        vch, sch = mk_chains(te, ve, se, TE, VE, SE, wt)
        id16 = e0(nc.sbuf_tensor("id16", [128, 128], F16))
        id32 = e0(nc.sbuf_tensor("id32", [128, 128], F32))
        cst = e0(nc.sbuf_tensor("cst", [128, 4], F32))
        gqb = e0(nc.sbuf_tensor("gqb_s", [128, D], F16))
        gkb = e0(nc.sbuf_tensor("gkb_s", [128, KVH * DH], F16))

        with ExitStack() as ctxY, ExitStack() as ctxZ, ExitStack() as ctxA:
            yT = ctxY.enter_context(nc.sbuf_tensor("yT", [128, 16, NT * 128], F16))
            zT = ctxZ.enter_context(nc.sbuf_tensor("zT", [128, H, NT * 128], F16))
            eA = ctxA.enter_context
            kT = eA(nc.sbuf_tensor("kT", [128, KVH, S], F16))
            vsb = eA(nc.sbuf_tensor("vsb", [128, NKT, KVH * DH], F16))

            _scid_P1_kv = nc.enter_named_scope("P1_kv", False)[0]
            with ExitStack() as c1:
                e1 = c1.enter_context
                wkv = e1(nc.sbuf_tensor("wkv", [128, 16, 1024], F16))
                wstg = [e1(nc.sbuf_tensor(f"wstg{i}", [128, 1024], F32)) for i in range(2)]
                xr32 = [e1(nc.sbuf_tensor(f"xr32{i}", [128, D], F32)) for i in range(2)]
                xr16 = e1(nc.sbuf_tensor("xr16", [128, D], F16))
                xTt = [e1(nc.sbuf_tensor(f"xTt{i}", [128, 16, 128], F16)) for i in range(2)]
                khat = e1(nc.sbuf_tensor("khat", [128, 512], F16))
                sqk = e1(nc.sbuf_tensor("sqk", [128, 512], F32))
                msk = e1(nc.sbuf_tensor("msk", [128, 4], F32))
                sdk = e1(nc.sbuf_tensor("sdk", [128, 4], F32))
                rk = e1(nc.sbuf_tensor("rk", [128, 4], F32))
                psk = [e1(nc.psum_tensor(f"psk{i}", [128, 512], F32)) for i in range(2)]
                psv = [e1(nc.psum_tensor(f"psv{i}", [128, 512], F32)) for i in range(2)]
                ptk = e1(nc.psum_tensor("ptk", [128, 512], F16))
                ptx = [e1(nc.psum_tensor(f"ptx{i}", [128, 512], F16)) for i in range(2)]
                pjk1 = e1(nc.psum_tensor("pjk1", [128, 128], F16))

                VE.memset(cst[:, 0:1], EPS_RMS)
                VE.memset(cst[:, 1:2], 1.0 / DH)
                VE.memset(cst[:, 2:3], DH * EPS_RMS)
                cs_done = ve.tick(VE.memset(cst[:, 3:4], EPS_LN))

                dg.tick(SY.dma_start(id16[:], ident16))
                dg.tick(SY.dma_start(id32[:], ident32))
                dg.tick(SY.dma_start(wstg[0][:], gqb_d[:, 0:1024]))
                dg.tick(SY.dma_start(wstg[1][:], gqb_d[:, 1024:2048]))
                idr = dg.tick(SY.dma_start(xr32[0][:, 0:512], gkb_d))
                wt(VE, dg, idr)
                VE.tensor_copy(gqb[:, 0:1024], wstg[0][:])
                VE.tensor_copy(gqb[:, 1024:2048], wstg[1][:])
                gam_done = ve.tick(VE.tensor_copy(gkb[:], xr32[0][:, 0:512]))

                wkv_cast = [0, 0]
                wr = [0, 0]

                def issue_wkv(dc):
                    p = dc % 2
                    wt(SY, ve, wkv_cast[p])
                    swkv[p].tick(SY.dma_start(wstg[p][:, 0:512],
                                              wk_d[128 * dc:128 * (dc + 1), :]))
                    wr[p] = swkv[p].tick(
                        SY.dma_start(wstg[p][:, 512:1024],
                                     wv_d[128 * dc:128 * (dc + 1), :]))

                wt(SY, ve, gam_done)
                issue_wkv(0)
                for dc in range(16):
                    p = dc % 2
                    if dc + 1 < 16:
                        issue_wkv(dc + 1)
                    wt(VE, swkv[p], wr[p])
                    wkv_cast[p] = ve.tick(VE.tensor_copy(wkv[:, dc, :], wstg[p][:]))
                wkv_ready = max(wkv_cast)

                xr_ready = [0, 0]
                cast_done = [0, 0]
                xr16_free = 0
                xt_used = [0, 0]
                ptx_ev = [0, 0]
                se_apply_done = [0, 0]
                ve_vcopy_done = [0, 0]
                kt_ev_done = 0

                wt(SE, ve, gam_done)
                xr_ready[0] = sxr[0].tick(SE.dma_start(xr32[0][:], x_d[0:128, :]))
                for kt in range(NKT):
                    p = kt % 2
                    if kt + 1 < NKT:
                        pn = (kt + 1) % 2
                        wt(SE, ve, cast_done[pn])
                        xr_ready[pn] = sxr[pn].tick(
                            SE.dma_start(xr32[pn][:], x_d[128 * (kt + 1):128 * (kt + 2), :]))
                    wt(VE, sxr[p], xr_ready[p])
                    wt(VE, te, xr16_free)
                    cast_done[p] = ve.tick(VE.tensor_copy(xr16[:], xr32[p][:]))
                    wt(TE, ve, cast_done[p])
                    wt(TE, dg, idr)
                    tks = []
                    for grp in range(4):
                        gp_ = grp % 2
                        wt(TE, se, ptx_ev[gp_])
                        tr = None
                        for u in range(4):
                            dc = 4 * grp + u
                            tr = TE.transpose(ptx[gp_][:, 128 * u:128 * (u + 1)],
                                              xr16[:, 128 * dc:128 * (dc + 1)], id16[:])
                        tks.append(te.tick(tr))
                        if grp >= 1:
                            wt(SE, te, tks[grp])
                            wt(SE, te, xt_used[p])
                            gq_ = (grp - 1) % 2
                            ptx_ev[gq_] = se.tick(
                                SE.copy(xTt[p][:, 4 * (grp - 1):4 * grp, :], ptx[gq_][:]))
                    tks.append(te.tick(TE.transpose(pjk1[:], id16[:], id16[:])))
                    wt(SE, te, tks[4])
                    wt(SE, te, xt_used[p])
                    ptx_ev[1] = se.tick(
                        SE.copy(xTt[p][:, 12:16, :], ptx[1][:]))
                    xr16_free = te.n
                    xt_ev = se.n
                    wt(TE, se, se_apply_done[p])
                    wt(TE, ve, ve_vcopy_done[p])
                    wt(TE, ve, wkv_ready)
                    wt(TE, se, xt_ev)
                    mmv = None
                    for dc in range(16):
                        lhsT = xTt[p][:, dc, :]
                        TE.matmul(psk[p][:], lhsT, wkv[:, dc, 0:512],
                                  start=(dc == 0), stop=(dc == 15))
                        mmv = TE.matmul(psv[p][:], lhsT, wkv[:, dc, 512:1024],
                                        start=(dc == 0), stop=(dc == 15))
                    mm_done = te.tick(mmv)
                    xt_used[p] = mm_done
                    wt(SE, te, mm_done)
                    sq_done = se.tick(SE.activation(sqk[:], psk[p][:], ACT.Square))
                    wt(VE, se, sq_done)
                    rs = None
                    for s4 in range(4):
                        rs = VE.reduce_sum(msk[:, s4:s4 + 1],
                                           sqk[:, 128 * s4:128 * (s4 + 1)], axis=AX)
                    red_done = ve.tick(rs)
                    wt(SE, ve, red_done)
                    wt(SE, ve, cs_done)
                    sd_done = se.tick(SE.activation(sdk[:], msk[:], ACT.Sqrt,
                                                    bias=cst[:, 0:1], scale=cst[:, 1:2]))
                    wt(VE, se, sd_done)
                    rk_done = ve.tick(VE.reciprocal(rk[:], sdk[:]))
                    wt(SE, ve, rk_done)
                    ap = None
                    for s4 in range(4):
                        ap = SE.mul(khat[:, 128 * s4:128 * (s4 + 1)],
                                    psk[p][:, 128 * s4:128 * (s4 + 1)], rk[:, s4:s4 + 1])
                    se_apply_done[p] = se.tick(ap)
                    wt(VE, se, se_apply_done[p])
                    kg_done = ve.tick(VE.tensor_mul(khat[:], khat[:], gkb[:]))
                    wt(TE, ve, kg_done)
                    wt(TE, se, kt_ev_done)
                    for s4 in range(4):
                        TE.transpose(ptk[:, 128 * s4:128 * (s4 + 1)],
                                     khat[:, 128 * s4:128 * (s4 + 1)], id16[:])
                    tr_done = te.tick(TE.transpose(pjk1[:], id16[:], id16[:]))
                    wt(SE, te, tr_done)
                    cp = None
                    for s4 in range(4):
                        cp = SE.copy(kT[:, s4, 128 * kt:128 * (kt + 1)],
                                     ptk[:, 128 * s4:128 * (s4 + 1)])
                    kt_ev_done = se.tick(cp)
                    wt(VE, te, mm_done)
                    ve_vcopy_done[p] = ve.tick(VE.tensor_copy(vsb[:, kt, :], psv[p][:]))

            nc.leave_named_scope("P1_kv", _scid_P1_kv, False)
            _scid_P2_q = nc.enter_named_scope("P2_q", False)[0]
            for _E in (SY, SE):
                wt(_E, te, te.n)
                wt(_E, ve, ve.n)
                wt(_E, se, se.n)
            qT = eA(nc.sbuf_tensor("qT", [128, H, NT * 128], F16))
            with ExitStack() as c2:
                e2 = c2.enter_context
                qh16 = e2(nc.sbuf_tensor("qh16", [128, 512], F16))
                wqs = [e2(nc.sbuf_tensor(f"wqs{i}", [128, 512], F32)) for i in range(2)]
                xqs = e2(nc.sbuf_tensor("xqs", [128, 16, 512], F32))
                sq128 = e2(nc.sbuf_tensor("sq128", [128, 128], F32))
                msq = e2(nc.sbuf_tensor("msq", [128, 16], F32))
                sdq = e2(nc.sbuf_tensor("sdq", [128, 16], F32))
                rq = e2(nc.sbuf_tensor("rq", [128, 16], F32))
                psq = [e2(nc.psum_tensor(f"psq{i}", [128, 512], F32)) for i in range(4)]
                ptq = e2(nc.psum_tensor("ptq", [128, 512], F16))
                pjk2 = e2(nc.psum_tensor("pjk2", [128, 128], F16))

                xqs_ready = 0
                xqs_freed = 0
                wq_rd = [0, 0]
                wq_freed = [0, 0]
                se_qapply_done = [0, 0, 0, 0]
                rq_done = [0, 0, 0, 0]
                qh_free = 0
                ptq_ev = 0
                mmq_done = {}

                def issue_wq(fq, dc, after):
                    i = dc % 2
                    Q = SY if i == 0 else SE
                    wt(Q, te, after)
                    wq_rd[i] = swq[i].tick(
                        Q.dma_start(wqs[i][:],
                                    wq_d[128 * dc:128 * (dc + 1),
                                         512 * fq:512 * (fq + 1)]))
                    return wq_rd[i]

                for qd in range(2):
                    wt(SE, te, xqs_freed)
                    r = 0
                    for dc in range(16):
                        r = sxq.tick(
                            SE.dma_start(xqs[:, dc, :],
                                         xqT_d[128 * dc:128 * (dc + 1),
                                               512 * qd:512 * (qd + 1)]))
                    xqs_ready = r
                    for fq in range(4):
                        rd_at = {}
                        rd_at[0] = issue_wq(fq, 0, wq_freed[0])
                        rd_at[1] = issue_wq(fq, 1, wq_freed[1])
                        for dc in range(16):
                            wt(TE, swq[dc % 2], rd_at[dc])
                            wt(TE, sxq, xqs_ready)
                            mq = None
                            for j in range(4):
                                wt(TE, se, se_qapply_done[j])
                                mq = TE.matmul(psq[j][:],
                                               xqs[:, dc, 128 * j:128 * (j + 1)],
                                               wqs[dc % 2][:],
                                               start=(dc == 0), stop=(dc == 15))
                            mmq_done[dc] = te.tick(mq)
                            if dc + 2 < 16:
                                rd_at[dc + 2] = issue_wq(fq, dc + 2, mmq_done[dc])
                        wq_freed = [mmq_done[14], mmq_done[15]]
                        for j in range(4):
                            sl = 4 * qd + j
                            wt(SE, te, mmq_done[15])
                            wt(SE, ve, rq_done[j])
                            for s4 in range(4):
                                sch(SE.activation(sq128[:],
                                                  psq[j][:, 128 * s4:128 * (s4 + 1)],
                                                  ACT.Square,
                                                  accum_out=msq[:, 4 * j + s4:
                                                                4 * j + s4 + 1]))
                            sd_done = sch(SE.activation(sdq[:, 4 * j:4 * j + 4],
                                                        msq[:, 4 * j:4 * j + 4],
                                                        ACT.Sqrt,
                                                        bias=cst[:, 2:3], scale=1.0))
                            wt(VE, se, sd_done)
                            rq_done[j] = ve.tick(VE.reciprocal(rq[:, 4 * j:4 * j + 4],
                                                               sdq[:, 4 * j:4 * j + 4]))
                            wt(SE, ve, rq_done[j])
                            wt(SE, te, qh_free)
                            ap = None
                            for s4 in range(4):
                                ap = SE.mul(qh16[:, 128 * s4:128 * (s4 + 1)],
                                            psq[j][:, 128 * s4:128 * (s4 + 1)],
                                            rq[:, 4 * j + s4:4 * j + s4 + 1])
                            se_qapply_done[j] = se.tick(ap)
                            wt(VE, se, se_qapply_done[j])
                            qg_done = ve.tick(VE.tensor_mul(
                                qh16[:], qh16[:], gqb[:, 512 * fq:512 * (fq + 1)]))
                            wt(TE, ve, qg_done)
                            wt(TE, se, ptq_ev)
                            tr = None
                            for u in range(4):
                                tr = TE.transpose(ptq[:, 128 * u:128 * (u + 1)],
                                                  qh16[:, 128 * u:128 * (u + 1)],
                                                  id16[:])
                            te.tick(tr)
                            trd = te.tick(TE.transpose(pjk2[:], id16[:], id16[:]))
                            qh_free = trd
                            wt(SE, te, trd)
                            cp = None
                            for u in range(4):
                                hh = 4 * fq + u
                                cp = SE.copy(qT[:, hh, 128 * sl:128 * (sl + 1)],
                                             ptq[:, 128 * u:128 * (u + 1)])
                            ptq_ev = se.tick(cp)
                    xqs_freed = te.n

            nc.leave_named_scope("P2_q", _scid_P2_q, False)
            _scid_P3_attn = nc.enter_named_scope("P3_attn", False)[0]
            for _E in (SY, SE):
                wt(_E, te, te.n)
                wt(_E, ve, ve.n)
                wt(_E, se, se.n)
            p23_se = se.n
            p23_ve = ve.n
            wt(TE, se, p23_se)
            wt(TE, ve, p23_ve)
            with ExitStack() as c3:
                e3 = c3.enter_context
                scr = [e3(nc.sbuf_tensor(f"scr{i}", [128, 4096], F32)) for i in range(2)]
                ph = [e3(nc.sbuf_tensor(f"ph{i}", [128, 4096], F16)) for i in range(2)]
                pt4 = [e3(nc.sbuf_tensor(f"pt4{i}", [128, 512], F16)) for i in range(2)]
                dmb = [e3(nc.sbuf_tensor(f"dmb{i}", [128, 512], F32)) for i in range(2)]
                nm = [e3(nc.sbuf_tensor(f"nm{i}", [128, 1], F32)) for i in range(2)]
                ell = [e3(nc.sbuf_tensor(f"ell{i}", [128, 1], F32)) for i in range(2)]
                li = [e3(nc.sbuf_tensor(f"li{i}", [128, 1], F32)) for i in range(2)]
                pss = [e3(nc.psum_tensor(f"pss{i}", [128, 512], F32)) for i in range(2)]
                pst = [e3(nc.psum_tensor(f"pst{i}", [128, 512], F16)) for i in range(2)]
                pso = [e3(nc.psum_tensor(f"pso{i}", [128, 128], F32)) for i in range(2)]
                pjk3 = e3(nc.psum_tensor("pjk3", [128, 128], F16))

                NIT = NT * H
                def it_gh(i):
                    return i // H, i % H

                sc = [0, 0]
                ve_scr_done = [0, 0]
                nm_done = [0, 0]
                ex_done = [0, 0]
                te_ph_done = [0, 0]
                se_pt_done = [0, 0]
                zev_done = [0, 0]
                dm_ready = [0, 0]
                dm_used = [0, 0]
                dm_ready[0] = sdm[0].tick(SE.dma_start(dmb[0][:], dmask_d[0]))
                dm_issued = 0

                def emit_stats(i):
                    g, h = it_gh(i)
                    nb = g + 1
                    rho = i % 2
                    g4 = h // 4
                    for kb in range(nb):
                        p = kb % 2
                        wt(TE, se, sc[p])
                        mm = TE.matmul(pss[p][:], qT[:, h, 128 * g:128 * (g + 1)],
                                       kT[:, g4, 512 * kb:512 * (kb + 1)],
                                       start=True, stop=True)
                        mmd = te.tick(mm)
                        wt(SE, te, mmd)
                        wt(SE, ve, ve_scr_done[rho])
                        sc[p] = se.tick(SE.copy(scr[rho][:, 512 * kb:512 * (kb + 1)],
                                                pss[p][:]))

                def emit_sm_front(i):
                    g, h = it_gh(i)
                    nb = g + 1
                    rho = i % 2
                    wt(VE, se, max(sc))
                    wt(VE, sdm[g % 2], dm_ready[g % 2])
                    vch(VE.tensor_tensor(scr[rho][:, 512 * g:512 * (g + 1)],
                                         scr[rho][:, 512 * g:512 * (g + 1)],
                                         dmb[g % 2][:], op=OP.min))
                    nm_done[rho] = vch(VE.tensor_reduce(nm[rho][:],
                                                        scr[rho][:, 0:512 * nb],
                                                        axis=AX, op=OP.max, negate=True))
                    ve_scr_done[rho] = nm_done[rho]
                    dm_used[g % 2] = nm_done[rho]

                def emit_sm_back(i):
                    g, h = it_gh(i)
                    nb = g + 1
                    rho = i % 2
                    wt(SE, ve, nm_done[rho])
                    wt(SE, ve, li_done[rho])
                    wt(SE, te, te_ph_done[rho])
                    exp_done = sch(SE.activation(ph[rho][:, 0:512 * nb],
                                                 scr[rho][:, 0:512 * nb],
                                                 ACT.Exp, bias=nm[rho][:],
                                                 accum_out=ell[rho][:]))
                    wt(VE, se, exp_done)
                    li_done[rho] = vch(VE.reciprocal(li[rho][:], ell[rho][:]))
                    ex_done[rho] = vch(VE.tensor_scalar(ph[rho][:, 0:512 * nb],
                                                        ph[rho][:, 0:512 * nb],
                                                        li[rho][:, 0:1], 0.0,
                                                        op0=OP.mult, op1=OP.add))

                def emit_tav(i):
                    g, h = it_gh(i)
                    nb = g + 1
                    rho = i % 2
                    g4 = h // 4
                    EV = SE if (i % 2 == 0) else VE
                    evs = se if (i % 2 == 0) else ve
                    wt(TE, ve, ex_done[rho])
                    wt(TE, se if zev_eng[rho] == 0 else ve, zev_done[rho])

                    def emit_tchunk(q4):
                        pp = q4 % 2
                        wt(TE, se, se_pt_done[pp] if se_pt_eng[pp] == 0 else 0)
                        wt(TE, ve, se_pt_done[pp] if se_pt_eng[pp] == 1 else 0)
                        tr = None
                        for u in range(4):
                            tr = TE.transpose(pst[pp][:, 128 * u:128 * (u + 1)],
                                              ph[rho][:, 512 * q4 + 128 * u:
                                                       512 * q4 + 128 * (u + 1)], id16[:])
                        return te.tick(tr)

                    tch = [0] * (nb + 1)
                    tch[0] = emit_tchunk(0)
                    av_done = 0
                    for q4 in range(nb):
                        if q4 + 1 < nb:
                            tch[q4 + 1] = emit_tchunk(q4 + 1)
                        else:
                            tch[q4 + 1] = te.tick(
                                TE.transpose(pjk3[:], id16[:], id16[:]))
                        wt(EV, te, tch[q4 + 1])
                        if EV is SE:
                            se_pt_done[q4 % 2] = evs.tick(
                                SE.copy(pt4[q4 % 2][:], pst[q4 % 2][:]))
                            se_pt_eng[q4 % 2] = 0
                        else:
                            se_pt_done[q4 % 2] = evs.tick(
                                VE.tensor_copy(pt4[q4 % 2][:], pst[q4 % 2][:]))
                            se_pt_eng[q4 % 2] = 1
                        wt(TE, se if se_pt_eng[q4 % 2] == 0 else ve,
                           se_pt_done[q4 % 2])
                        av = None
                        for u in range(4):
                            ktile = 4 * q4 + u
                            av = TE.matmul(pso[rho][:],
                                           vsb[:, ktile, 128 * g4:128 * (g4 + 1)],
                                           pt4[q4 % 2][:, 128 * u:128 * (u + 1)],
                                           start=(ktile == 0),
                                           stop=(ktile == 4 * nb - 1))
                        av_done = te.tick(av)
                    te_ph_done[rho] = te.n
                    wt(EV, te, av_done)
                    if EV is SE:
                        zev_done[rho] = evs.tick(
                            SE.copy(zT[:, h, 128 * g:128 * (g + 1)], pso[rho][:]))
                        zev_eng[rho] = 0
                    else:
                        zev_done[rho] = evs.tick(
                            VE.tensor_copy(zT[:, h, 128 * g:128 * (g + 1)], pso[rho][:]))
                        zev_eng[rho] = 1

                se_pt_eng = [0, 0]
                zev_eng = [0, 0]
                li_done = [0, 0]
                emit_stats(0)
                emit_stats(1)
                emit_sm_front(0)
                for i in range(NIT):
                    g, h = it_gh(i)
                    if h == 0 and g + 1 < NT and dm_issued < g + 1:
                        pn = (g + 1) % 2
                        wt(SE, ve, dm_used[pn])
                        dm_ready[pn] = sdm[pn].tick(
                            SE.dma_start(dmb[pn][:], dmask_d[g + 1]))
                        dm_issued = g + 1
                    if i + 1 < NIT:
                        emit_sm_front(i + 1)
                    emit_sm_back(i)
                    if i + 2 < NIT:
                        emit_stats(i + 2)
                    emit_tav(i)
            zev_all = se.n
            ctxA.close()

            nc.leave_named_scope("P3_attn", _scid_P3_attn, False)
            _scid_P4_wo = nc.enter_named_scope("P4_wo", False)[0]
            for _E in (SY, SE):
                wt(_E, te, te.n)
                wt(_E, ve, ve.n)
                wt(_E, se, se.n)
            with ExitStack() as c4:
                e4 = c4.enter_context
                wo16 = e4(nc.sbuf_tensor("wo16", [128, 16, D], F16))
                wos = [e4(nc.sbuf_tensor(f"wos{i}", [128, D], F32)) for i in range(2)]
                xts = [e4(nc.sbuf_tensor(f"xts{i}", [128, D], F32)) for i in range(2)]
                rbuf = e4(nc.sbuf_tensor("rbuf", [128, D], F32))
                jnk = e4(nc.sbuf_tensor("jnk", [128, D], F32))
                ytile = e4(nc.sbuf_tensor("ytile", [128, D], F16))
                s1 = e4(nc.sbuf_tensor("s1", [128, 1], F32))
                s2 = e4(nc.sbuf_tensor("s2", [128, 1], F32))
                mu = e4(nc.sbuf_tensor("mu", [128, 1], F32))
                m2m = e4(nc.sbuf_tensor("m2m", [128, 1], F32))
                msq2 = e4(nc.sbuf_tensor("msq2", [128, 1], F32))
                var = e4(nc.sbuf_tensor("var", [128, 1], F32))
                sd = e4(nc.sbuf_tensor("sd", [128, 1], F32))
                rstd = e4(nc.sbuf_tensor("rstd", [128, 1], F32))
                nmr = e4(nc.sbuf_tensor("nmr", [128, 1], F32))
                pswo = e4(nc.psum_tensor("pswo", [128, D], F32))
                psyt = [e4(nc.psum_tensor(f"psyt{i}", [128, 512], F16)) for i in range(2)]
                pjk4 = e4(nc.psum_tensor("pjk4", [128, 128], F16))

                wo_cast = [0, 0]
                wo_rd = [0, 0]

                def issue_wo(dc):
                    p = dc % 2
                    wt(SY, ve, wo_cast[p])
                    wo_rd[p] = swo[p].tick(
                        SY.dma_start(wos[p][:], wo_d[128 * dc:128 * (dc + 1), :]))

                issue_wo(0)
                for dc in range(16):
                    p = dc % 2
                    if dc + 1 < 16:
                        issue_wo(dc + 1)
                    wt(VE, swo[p], wo_rd[p])
                    wo_cast[p] = ve.tick(VE.tensor_copy(wo16[:, dc, :], wos[p][:]))
                wo_all = max(wo_cast)

                xq_ready = [0, 0]
                xq_ready[0] = sxt[0].tick(SE.dma_start(xts[0][:], xq_d[0:128, :]))

                ve_r_done = 0
                yt_te_done = 0
                yt_ev = [0, 0]
                for g in range(NT):
                    if g + 1 < NT:
                        pn = (g + 1) % 2
                        wt(SE, ve, ve_r_done)
                        xq_ready[pn] = sxt[pn].tick(
                            SE.dma_start(xts[pn][:], xq_d[128 * (g + 1):128 * (g + 2), :]))
                    wt(TE, ve, wo_all)
                    wt(TE, se, zev_all)
                    wt(TE, ve, ve_r_done)
                    mo = None
                    for hh in range(16):
                        for nbk in range(4):
                            mo = TE.matmul(pswo[:, 512 * nbk:512 * (nbk + 1)],
                                           zT[:, hh, 128 * g:128 * (g + 1)],
                                           wo16[:, hh, 512 * nbk:512 * (nbk + 1)],
                                           start=(hh == 0), stop=(hh == 15))
                    wo_mm = te.tick(mo)
                    wt(VE, te, wo_mm)
                    wt(VE, sxt[g % 2], xq_ready[g % 2])
                    vch(VE.scalar_tensor_tensor(rbuf[:], xts[g % 2][:], 2.0, pswo[:],
                                                op0=OP.mult, op1=OP.add))
                    vch(VE.tensor_scalar(jnk[:], rbuf[:], 1.0, 0.0, op0=OP.mult,
                                         op1=OP.add, accum_out=s1[:]))
                    vch(VE.scalar_tensor_tensor(jnk[:], rbuf[:], 1.0, rbuf[:],
                                                op0=OP.mult, op1=OP.mult, accum_out=s2[:]))
                    vch(VE.tensor_scalar(mu[:], s1[:], 1.0 / D, 0.0, op0=OP.mult, op1=OP.add))
                    vch(VE.tensor_scalar(m2m[:], s2[:], 1.0 / D, 0.0, op0=OP.mult, op1=OP.add))
                    vch(VE.tensor_mul(msq2[:], mu[:], mu[:]))
                    ve_r_done = ve.tick(VE.tensor_sub(var[:], m2m[:], msq2[:]))
                    wt(SE, ve, ve_r_done)
                    sd_done = se.tick(SE.activation(sd[:], var[:], ACT.Sqrt,
                                                    bias=cst[:, 3:4]))
                    wt(VE, se, sd_done)
                    vch(VE.reciprocal(rstd[:], sd[:]))
                    nmr_done = vch(VE.tensor_scalar(nmr[:], mu[:], rstd[:, 0:1], -1.0,
                                                    op0=OP.mult, op1=OP.mult))
                    wt(VE, te, yt_te_done)
                    ve_y_done = ve.tick(VE.tensor_scalar(ytile[:], rbuf[:],
                                                         rstd[:, 0:1], nmr[:, 0:1],
                                                         op0=OP.mult, op1=OP.add))
                    wt(TE, ve, ve_y_done)
                    ytk = []
                    for grp in range(4):
                        pp = grp % 2
                        wt(TE, se, yt_ev[pp])
                        tr = None
                        for u in range(4):
                            cc = 4 * grp + u
                            tr = TE.transpose(psyt[pp][:, 128 * u:128 * (u + 1)],
                                              ytile[:, 128 * cc:128 * (cc + 1)], id16[:])
                        ytk.append(te.tick(tr))
                        if grp >= 1:
                            wt(SE, te, ytk[grp])
                            cp = None
                            for u2 in range(4):
                                c0 = 4 * (grp - 1) + u2
                                cp = SE.copy(yT[:, c0, 128 * g:128 * (g + 1)],
                                             psyt[(grp - 1) % 2][:, 128 * u2:128 * (u2 + 1)])
                            yt_ev[(grp - 1) % 2] = se.tick(cp)
                    ytk.append(te.tick(TE.transpose(pjk4[:], id16[:], id16[:])))
                    wt(SE, te, ytk[4])
                    cp = None
                    for u2 in range(4):
                        c0 = 12 + u2
                        cp = SE.copy(yT[:, c0, 128 * g:128 * (g + 1)],
                                     psyt[1][:, 128 * u2:128 * (u2 + 1)])
                    yt_ev[1] = se.tick(cp)
                    yt_te_done = te.n
                yT_all = se.n
            ctxZ.close()

            nc.leave_named_scope("P4_wo", _scid_P4_wo, False)
            _scid_P5_ffn = nc.enter_named_scope("P5_ffn", False)[0]
            for _E in (SY, SE):
                wt(_E, te, te.n)
                wt(_E, ve, ve.n)
                wt(_E, se, se.n)
            f = ctxY.enter_context(nc.sbuf_tensor("f", [128, NT, D], F32))
            with ExitStack() as c5m:
                e5 = c5m.enter_context
                gatedT = e5(nc.sbuf_tensor("gatedT", [128, 8, 1024], F16))
                wd16 = e5(nc.sbuf_tensor("wd16", [128, 8, D], F16))
                gsil = e5(nc.sbuf_tensor("gsil", [128, 1024], F16))
                wgs32 = [e5(nc.sbuf_tensor(f"wgs32{i}", [128, 16, 128], F32)) for i in range(2)]
                wus32 = [e5(nc.sbuf_tensor(f"wus32{i}", [128, 16, 128], F32)) for i in range(2)]
                wgs16 = [e5(nc.sbuf_tensor(f"wgs16{i}", [128, 16, 128], F16)) for i in range(2)]
                wus16 = [e5(nc.sbuf_tensor(f"wus16{i}", [128, 16, 128], F16)) for i in range(2)]
                wds = e5(nc.sbuf_tensor("wds", [128, D], F32))
                psg = e5(nc.psum_tensor("psg", [128, 1024], F32))
                psu = e5(nc.psum_tensor("psu", [128, 1024], F32))
                psd = e5(nc.psum_tensor("psd", [128, D], F32))

                w_rd = [0, 0]
                w_cast = [0, 0]
                w16_used = [0, 0]
                wd_rd = 0
                wd_cast = 0
                silu_done = 0
                mult_done = 0
                gated_free = 0
                slice_reader = {}
                f_done = {}

                def issue_w(eht):
                    p = eht % 2
                    wt(SY, se, w_cast[p])
                    r = 0
                    for dc in range(16):
                        swgu[p].tick(
                            SY.dma_start(wgs32[p][:, dc, :],
                                         wgT_d[128 * dc:128 * (dc + 1),
                                               128 * eht:128 * (eht + 1)]))
                        r = swgu[p].tick(
                            SY.dma_start(wus32[p][:, dc, :],
                                         wuT_d[128 * dc:128 * (dc + 1),
                                               128 * eht:128 * (eht + 1)]))
                    w_rd[p] = r

                issue_w(0)
                for e8 in range(8):
                    for ht in range(8):
                        eht = 8 * e8 + ht
                        p = eht % 2
                        if eht + 1 < 64:
                            issue_w(eht + 1)
                        wt(SE, ve, wd_cast)
                        wd_rd = swd.tick(
                            SE.dma_start(wds[:], wdT_d[128 * eht:128 * (eht + 1), :]))
                        wt(SE, swgu[p], w_rd[p])
                        wt(SE, te, w16_used[p])
                        SE.copy(wgs16[p][:], wgs32[p][:])
                        w_cast[p] = se.tick(SE.copy(wus16[p][:], wus32[p][:]))
                        wt(TE, se, w_cast[p])
                        wt(TE, se, yT_all)
                        wt(TE, se, silu_done)
                        mg = None
                        for dc in range(16):
                            mg = TE.matmul(psg[:, 0:512], wgs16[p][:, dc, :],
                                           yT[:, dc, 0:512],
                                           start=(dc == 0), stop=(dc == 15))
                            mg = TE.matmul(psg[:, 512:1024], wgs16[p][:, dc, :],
                                           yT[:, dc, 512:1024],
                                           start=(dc == 0), stop=(dc == 15))
                        gate_mm = te.tick(mg)
                        wt(TE, ve, mult_done)
                        mup = None
                        for dc in range(16):
                            mup = TE.matmul(psu[:, 0:512], wus16[p][:, dc, :],
                                            yT[:, dc, 0:512],
                                            start=(dc == 0), stop=(dc == 15))
                            mup = TE.matmul(psu[:, 512:1024], wus16[p][:, dc, :],
                                            yT[:, dc, 512:1024],
                                            start=(dc == 0), stop=(dc == 15))
                        up_mm = te.tick(mup)
                        w16_used[p] = up_mm
                        wt(SE, te, gate_mm)
                        wt(SE, ve, mult_done)
                        silu_done = se.tick(SE.activation(gsil[:], psg[:], ACT.Silu))
                        wt(VE, se, silu_done)
                        wt(VE, te, up_mm)
                        wt(VE, te, gated_free)
                        mult_done = ve.tick(VE.tensor_mul(gatedT[:, ht, :],
                                                          gsil[:], psu[:]))
                        wt(VE, swd, wd_rd)
                        wd_cast = ve.tick(VE.tensor_copy(wd16[:, ht, :], wds[:]))
                    wt(TE, ve, mult_done)
                    wt(TE, ve, wd_cast)
                    for tt in range(NT):
                        for dout in range(4):
                            wt(TE, ve, slice_reader.get(dout, 0))
                            md = None
                            for ht2 in range(8):
                                md = TE.matmul(psd[:, 512 * dout:512 * (dout + 1)],
                                               gatedT[:, ht2, 128 * tt:128 * (tt + 1)],
                                               wd16[:, ht2, 512 * dout:512 * (dout + 1)],
                                               start=(ht2 == 0), stop=(ht2 == 7))
                            dmm = te.tick(md)
                            wt(VE, te, dmm)
                            wt(VE, ve, f_done.get((tt, dout), 0))
                            if e8 == 0:
                                fi = VE.tensor_copy(f[:, tt, 512 * dout:512 * (dout + 1)],
                                                    psd[:, 512 * dout:512 * (dout + 1)])
                            else:
                                fi = VE.tensor_add(f[:, tt, 512 * dout:512 * (dout + 1)],
                                                   f[:, tt, 512 * dout:512 * (dout + 1)],
                                                   psd[:, 512 * dout:512 * (dout + 1)])
                            f_done[(tt, dout)] = ve.tick(fi)
                            slice_reader[dout] = f_done[(tt, dout)]
                    gated_free = te.n
                f_all = max(f_done.values())

            nc.leave_named_scope("P5_ffn", _scid_P5_ffn, False)
            _scid_P6_final = nc.enter_named_scope("P6_final", False)[0]
            for _E in (SY, SE):
                wt(_E, te, te.n)
                wt(_E, ve, ve.n)
                wt(_E, se, se.n)
            with ExitStack() as c5f:
                e5f = c5f.enter_context
                psy = e5f(nc.psum_tensor("psy", [128, D], F16))
                pjk5 = e5f(nc.psum_tensor("pjk5", [128, 128], F16))
                ybk = e5f(nc.sbuf_tensor("ybk", [128, D], F32))
                r2 = e5f(nc.sbuf_tensor("r2", [128, D], F32))
                jnk2 = e5f(nc.sbuf_tensor("jnk2", [128, D], F32))
                ot = [e5f(nc.sbuf_tensor(f"ot{i}", [128, D], F32)) for i in range(2)]
                s1b = e5f(nc.sbuf_tensor("s1b", [128, 1], F32))
                s2b = e5f(nc.sbuf_tensor("s2b", [128, 1], F32))
                mub = e5f(nc.sbuf_tensor("mub", [128, 1], F32))
                m2b = e5f(nc.sbuf_tensor("m2b", [128, 1], F32))
                msqb = e5f(nc.sbuf_tensor("msqb", [128, 1], F32))
                varb = e5f(nc.sbuf_tensor("varb", [128, 1], F32))
                sdb = e5f(nc.sbuf_tensor("sdb", [128, 1], F32))
                rstdb = e5f(nc.sbuf_tensor("rstdb", [128, 1], F32))
                nmrb = e5f(nc.sbuf_tensor("nmrb", [128, 1], F32))

                se_yb_done = 0
                ve_r2_done = 0
                oy = 0
                store_done = {}
                for tt in range(NT):
                    wt(TE, se, se_yb_done)
                    wt(TE, ve, f_all)
                    for cc in range(16):
                        TE.transpose(psy[:, 128 * cc:128 * (cc + 1)],
                                     yT[:, cc, 128 * tt:128 * (tt + 1)], id16[:])
                    yb_done = te.tick(TE.transpose(pjk5[:], id16[:], id16[:]))
                    wt(SE, te, yb_done)
                    wt(SE, ve, ve_r2_done)
                    se_yb_done = se.tick(SE.copy(ybk[:], psy[:]))
                    wt(VE, se, se_yb_done)
                    wt(VE, ve, oy)
                    vch(VE.tensor_add(r2[:], f[:, tt, :], ybk[:]))
                    vch(VE.tensor_scalar(jnk2[:], r2[:], 1.0, 0.0, op0=OP.mult,
                                         op1=OP.add, accum_out=s1b[:]))
                    vch(VE.scalar_tensor_tensor(jnk2[:], r2[:], 1.0, r2[:],
                                                op0=OP.mult, op1=OP.mult, accum_out=s2b[:]))
                    vch(VE.tensor_scalar(mub[:], s1b[:], 1.0 / D, 0.0, op0=OP.mult, op1=OP.add))
                    vch(VE.tensor_scalar(m2b[:], s2b[:], 1.0 / D, 0.0, op0=OP.mult, op1=OP.add))
                    vch(VE.tensor_mul(msqb[:], mub[:], mub[:]))
                    ve_r2_done = ve.tick(VE.tensor_sub(varb[:], m2b[:], msqb[:]))
                    wt(SE, ve, ve_r2_done)
                    sdd = se.tick(SE.activation(sdb[:], varb[:], ACT.Sqrt,
                                                bias=cst[:, 3:4]))
                    wt(VE, se, sdd)
                    vch(VE.reciprocal(rstdb[:], sdb[:]))
                    nmd = vch(VE.tensor_scalar(nmrb[:], mub[:], rstdb[:, 0:1], -1.0,
                                               op0=OP.mult, op1=OP.mult))
                    if tt >= 2:
                        wt(VE, sst[tt % 2], store_done[tt - 2])
                    oy = ve.tick(VE.tensor_scalar(ot[tt % 2][:], r2[:],
                                                  rstdb[:, 0:1], nmrb[:, 0:1],
                                                  op0=OP.mult, op1=OP.add))
                    wt(SE, ve, oy)
                    store_done[tt] = sst[tt % 2].tick(
                        SE.dma_start(out_d[128 * tt:128 * (tt + 1), :], ot[tt % 2][:]))
                wt(SE, sst[0], store_done[NT - 2])
                wt(SE, sst[1], store_done[NT - 1])
            nc.leave_named_scope("P6_final", _scid_P6_final, False)
    return nc


def kernel(L0_attn_x, L0_attn_Wq, L0_attn_Wk, L0_attn_Wv, L0_attn_Wo,
           L0_attn_q_gamma, L0_attn_k_gamma,
           L0_ffn_w_gate, L0_ffn_w_up, L0_ffn_w_down):
    x = np.ascontiguousarray(np.asarray(L0_attn_x, dtype=np.float32))
    wq = np.ascontiguousarray(np.asarray(L0_attn_Wq, dtype=np.float32))
    wk = np.ascontiguousarray(np.asarray(L0_attn_Wk, dtype=np.float32))
    wv = np.ascontiguousarray(np.asarray(L0_attn_Wv, dtype=np.float32))
    wo = np.ascontiguousarray(np.asarray(L0_attn_Wo, dtype=np.float32))
    gq = np.asarray(L0_attn_q_gamma, dtype=np.float32)
    gk = np.asarray(L0_attn_k_gamma, dtype=np.float32)
    wgT = np.ascontiguousarray(np.asarray(L0_ffn_w_gate, dtype=np.float32).T)
    wuT = np.ascontiguousarray(np.asarray(L0_ffn_w_up, dtype=np.float32).T)
    wdT = np.ascontiguousarray(np.asarray(L0_ffn_w_down, dtype=np.float32).T)

    gqb = np.ascontiguousarray(
        np.broadcast_to(np.tile(gq, H), (128, D)).astype(np.float32))
    gkb = np.ascontiguousarray(
        np.broadcast_to(np.tile(gk, KVH), (128, KVH * DH)).astype(np.float32))

    in_maps = []
    metas = []
    for c in range(8):
        b = c % 2
        tpos = [4 * g + c // 2 for g in range(NT)]
        rows = np.concatenate([np.arange(128 * t, 128 * (t + 1)) for t in tpos])
        xq = np.ascontiguousarray(x[b][rows])
        xqT = np.ascontiguousarray(xq.T)
        dmask = np.empty((NT, 128, 512), dtype=np.float32)
        for g in range(NT):
            t = tpos[g]
            qpos = 128 * t + np.arange(128)[:, None]
            kpos = 512 * g + np.arange(512)[None, :]
            dmask[g] = np.where(kpos <= qpos, 3.0e38, -3.0e38)
        in_maps.append({
            "x": x[b], "xq": xq, "xqT": xqT,
            "wq": wq, "wk": wk, "wv": wv, "wo": wo,
            "wgT": wgT, "wuT": wuT, "wdT": wdT,
            "gqb": gqb, "gkb": gkb, "dmask": dmask,
        })
        metas.append((b, tpos))

    nc = build_program()
    res = run_bass_kernel_spmd(nc, in_maps, core_ids=list(range(8)))

    out = np.empty((B, S, D), dtype=np.float32)
    for c in range(8):
        b, tpos = metas[c]
        o = res.results[c]["out"]
        for g in range(NT):
            t = tpos[g]
            out[b, 128 * t:128 * (t + 1), :] = o[128 * g:128 * (g + 1), :]
    return out
